# version 12
# baseline (speedup 1.0000x reference)
import contextlib
import numpy as np
import concourse.bass as bass
import concourse.mybir as mybir
from concourse.bass_utils import run_bass_kernel_spmd

F32 = mybir.dt.float32
BF16 = mybir.dt.bfloat16
AX = mybir.AxisListType.X
ALU = mybir.AluOpType
AF = mybir.ActivationFunctionType

D = 1024
T_SEQ = 4096
T_OWN = 2048
NE = 8
I2 = 4096
EPS = 1e-5
CLS = [[0, 3, 4, 7], [1, 2, 5, 6]]
EPAD = [8, 16, 24, 32]

ZERO = ("PE", 0)


class Plan:
    ENGS = ("sync", "scalar", "vector", "tensor", "gpsimd")
    ENG_OF_SEM = {"DV": "vector", "AC": "scalar"}

    def __init__(self):
        self.plan = {e: [] for e in self.ENGS}
        self.cnt = {}
        self.hwm = {}
        self.junk_fn = {}
        self.nofence = set()

    def op(self, eng, fn, waits=(), inc=None, inc_amt=1, no_fence=False):
        fenced = []
        for w in waits:
            if w is None:
                continue
            sem, tgt = w
            if tgt <= 0:
                continue
            peng = self.ENG_OF_SEM.get(sem)
            if peng is not None and peng != eng and (sem, tgt) not in self.nofence:
                if self.cnt.get(sem, 0) == tgt and sem in self.junk_fn:
                    self.cnt[sem] = self.cnt.get(sem, 0) + 1
                    self.plan[peng].append((self.junk_fn[sem], [], sem, 1))
                if self.cnt.get(sem, 0) > tgt:
                    tgt += 1
            fenced.append((sem, tgt))
        pruned = []
        for sem, tgt in fenced:
            key = (eng, sem)
            if self.hwm.get(key, 0) >= tgt:
                continue
            self.hwm[key] = tgt
            pruned.append((sem, tgt))
        tok = None
        if inc is not None:
            self.cnt[inc] = self.cnt.get(inc, 0) + inc_amt
            tok = (inc, self.cnt[inc])
            if no_fence:
                self.nofence.add(tok)
        self.plan[eng].append((fn, pruned, inc, inc_amt))
        return tok

    def barrier(self):
        for e in self.ENGS:
            self.plan[e].append(("BARRIER", None, None, None))

    def emit(self, nc, sems):
        nseg = sum(1 for it in self.plan["sync"] if it[0] == "BARRIER") + 1
        pos = {e: 0 for e in self.ENGS}
        for seg in range(nseg):
            with nc.Block() as block:
                def mk(eng_name):
                    def run(eng):
                        items = self.plan[eng_name]
                        i = pos[eng_name]
                        while i < len(items) and items[i][0] != "BARRIER":
                            fn, waits, inc, amt = items[i]
                            for sem, tgt in waits:
                                eng.wait_ge(sems[sem], tgt)
                            inst = fn(eng)
                            if inc is not None:
                                inst.then_inc(sems[inc], amt)
                            i += 1
                        pos[eng_name] = i + 1
                    return run
                block.sync(mk("sync"))
                block.scalar(mk("scalar"))
                block.vector(mk("vector"))
                block.tensor(mk("tensor"))
                block.gpsimd(mk("gpsimd"))


def build_program(debug=False):
    nc = bass.Bass("TRN2", target_bir_lowering=False)
    IKIND = "ExternalOutput" if debug else "Internal"

    x_seq = nc.dram_tensor("x_seq", [T_SEQ, D], F32, kind="ExternalInput").ap()
    x_own = nc.dram_tensor("x_own", [T_OWN, D], F32, kind="ExternalInput").ap()
    wq_t = nc.dram_tensor("wq_t", [D, D], F32, kind="ExternalInput").ap()
    wk_t = nc.dram_tensor("wk_t", [D, D], F32, kind="ExternalInput").ap()
    wv_t = nc.dram_tensor("wv_t", [D, D], F32, kind="ExternalInput").ap()
    wout_t = nc.dram_tensor("wout_t", [D, D], F32, kind="ExternalInput").ap()
    bout = nc.dram_tensor("bout", [1, D], F32, kind="ExternalInput").ap()
    ns = nc.dram_tensor("ns", [1, D], F32, kind="ExternalInput").ap()
    gw = nc.dram_tensor("gw", [128, 8, NE], F32, kind="ExternalInput").ap()
    gb = nc.dram_tensor("gb", [1, NE], F32, kind="ExternalInput").ap()
    w1 = nc.dram_tensor("w1", [NE, D, I2], F32, kind="ExternalInput").ap()
    b1c = nc.dram_tensor("b1c", [NE, 128, 32], F32, kind="ExternalInput").ap()
    w2 = nc.dram_tensor("w2", [NE, 2048, D], F32, kind="ExternalInput").ap()
    b2 = nc.dram_tensor("b2", [NE, D], F32, kind="ExternalInput").ap()
    masksT = nc.dram_tensor("masks", [4, 8, 128, 512], F32, kind="ExternalInput").ap()
    ident = nc.dram_tensor("ident", [128, 128], F32, kind="ExternalInput").ap()
    out = nc.dram_tensor("out", [T_OWN, D], F32, kind="ExternalOutput").ap()

    x2s = nc.dram_tensor("x2s", [T_OWN, D], F32, kind=IKIND).ap()
    att_out = nc.dram_tensor("att_out", [1024, T_OWN], BF16, kind=IKIND).ap()
    y_dram = nc.dram_tensor("y_dram", [T_OWN, D], F32, kind=IKIND).ap()
    if debug:
        dbg_ln = nc.dram_tensor("dbg_ln", [128, 8, T_SEQ], BF16,
                                kind="ExternalOutput").ap()
        dbg_ln2 = nc.dram_tensor("dbg_ln2", [128, 8, T_SEQ], BF16,
                                 kind="ExternalOutput").ap()
        dbg_lno = nc.dram_tensor("dbg_lno", [128, 8, T_OWN], BF16,
                                 kind="ExternalOutput").ap()
        dbg_k = nc.dram_tensor("dbg_k", [128, 4, T_SEQ], BF16,
                               kind="ExternalOutput").ap()
        dbg_q = nc.dram_tensor("dbg_q", [128, 4, T_OWN], BF16,
                               kind="ExternalOutput").ap()
        dbg_a = nc.dram_tensor("dbg_a", [128, 1024], BF16,
                               kind="ExternalOutput").ap()
        dbg_pj = nc.dram_tensor("dbg_pj", [128, 1024], F32,
                                kind="ExternalOutput").ap()
        dbg_ao = nc.dram_tensor("dbg_ao", [128, 8, T_OWN], BF16,
                                kind="ExternalOutput").ap()
        dbg_ws = nc.dram_tensor("dbg_ws", [128, 16], F32,
                                kind="ExternalOutput").ap()
        dbg_t = nc.dram_tensor("dbg_t", [128, 8, T_OWN], BF16,
                               kind="ExternalOutput").ap()
        dbg_wst = nc.dram_tensor("dbg_wst", [128, 16, NE], F32,
                                 kind="ExternalOutput").ap()

    p = Plan()
    lane_rr = {"sync": 0, "gpsimd": 0}

    def DMA(queue, o, i, waits=(), **kw):
        pre = "dS" if queue == "sync" else "dQ"
        sem = f"{pre}{lane_rr[queue] % 8}"
        lane_rr[queue] += 1
        def fn(e, o=o, i=i, kw=kw):
            return e.dma_start(out=o, in_=i, **kw)
        return p.op(queue, fn, waits, sem, 16)

    def PE(fn, waits=()):
        return p.op("tensor", fn, waits, "PE")

    def DV(fn, waits=(), big=False):
        return p.op("vector", fn, waits, "DV", no_fence=big)

    def AC(fn, waits=(), big=False):
        return p.op("scalar", fn, waits, "AC", no_fence=big)

    def MM(o, l, r, start, stop, waits=()):
        def fn(e, o=o, l=l, r=r, start=start, stop=stop):
            return e.matmul(o, l, r, start=start, stop=stop)
        return PE(fn, waits)

    ctx = contextlib.ExitStack()
    sem_names = (["PE", "AC", "DV", "FN"] + [f"dS{i}" for i in range(8)]
                 + [f"dQ{i}" for i in range(8)])
    sems = {n: ctx.enter_context(nc.semaphore(n)) for n in sem_names}
    sb = lambda *a, **k: ctx.enter_context(nc.sbuf_tensor(*a, **k))

    junkv = ctx.enter_context(nc.sbuf_tensor("junkv", [128, 320], F32))
    _jc = [0]

    def _junk_dv(e):
        _jc[0] += 1
        return e.memset(junkv[:, 0:256], float(_jc[0] % 13))

    def _junk_ac(e):
        _jc[0] += 1
        return e.activation(junkv[:, 256:320], junkv[:, 256:320], AF.Identity,
                            scale=float(_jc[0] % 11))

    p.junk_fn["DV"] = _junk_dv
    p.junk_fn["AC"] = _junk_ac
    DV(lambda e: e.memset(junkv[:, :], 0.0))

    def JD():
        return p.op("vector", p.junk_fn["DV"], (), "DV")

    def JA():
        return p.op("scalar", p.junk_fn["AC"], (), "AC")

    def FENCE(eng):
        sem = "DV" if eng == "vector" else "AC"
        jf = p.junk_fn[sem]
        tm = p.op(eng, jf, (), sem, no_fence=True)
        tf = p.op("sync", lambda e: e.nop(), [tm], "FN")
        return p.op(eng, jf, [tf], sem, no_fence=True)

    ident_sb = sb("ident_sb", [128, 128], BF16)
    ones_c1 = sb("ones_c1", [1, 128], BF16)
    ones128 = sb("ones128", [128, 1], BF16)
    bout_sb = sb("bout_sb", [1, D], BF16)
    gw_sb = sb("gw_sb", [128, 8, NE], BF16)
    gb_sb = sb("gb_sb", [1, NE], BF16)
    b1c_sb = sb("b1c_sb", [128, NE, 32], F32)
    b2_sb = sb("b2_sb", [NE, D], BF16)
    ln2st = sb("ln2st", [128, 16, 2], F32)
    w_st = sb("w_st", [128, 16, NE], F32)
    sm = sb("sm", [128, 96], F32)

    tok_ident = DMA("gpsimd", ident_sb[:, :], ident[:, :])
    tok_bout = DMA("gpsimd", bout_sb[:, :], bout[:, :])
    tok_gw = DMA("gpsimd", gw_sb[:, :, :], gw[:, :, :])
    tok_gb = DMA("gpsimd", gb_sb[:, :], gb[:, :])
    tok_b1c = DMA("gpsimd", b1c_sb[:, :, :], b1c[:, :, :])
    tok_b2 = DMA("gpsimd", b2_sb[:, :], b2[:, :])
    tok_o1 = DV(lambda e: e.memset(ones_c1[:, :], 1.0))
    tok_o128 = DV(lambda e: e.memset(ones128[:, :], 1.0))
    eps128 = sb("eps128", [128, 1], F32)
    tok_eps = DV(lambda e: e.memset(eps128[:, :], EPS))
    epsq128 = sb("epsq128", [128, 1], F32)
    tok_epsq = DV(lambda e: e.memset(epsq128[:, :], EPS * EPS))
    tok_b1p = DV(lambda e: e.tensor_scalar_add(
        b1c_sb[:, :, 16:32], b1c_sb[:, :, 16:32], 1.0), [tok_b1c])

    wrm = sb("wrm", [128, 512], F32)
    t_w = DV(lambda e: e.memset(wrm[:, :], 1.0))
    for _i in range(40):
        t_w = AC(lambda e: e.activation(wrm[:, :], wrm[:, :],
                                        AF.Gelu_apprx_sigmoid), [t_w], big=True)
    for _i in range(20):
        t_w = AC(lambda e: e.activation(wrm[:, :], wrm[:, :], AF.Ln, bias=1.0),
                 [t_w], big=True)
        t_w = AC(lambda e: e.activation(wrm[:, :], wrm[:, :], AF.Exp, scale=0.1),
                 [t_w], big=True)
    tok_warm = t_w

    p.barrier()
    lnctx = contextlib.ExitStack()
    ln1_seq = lnctx.enter_context(nc.sbuf_tensor("ln1_seq", [128, 8, T_SEQ], BF16))
    ln1_own = lnctx.enter_context(nc.sbuf_tensor("ln1_own", [128, 8, T_OWN], BF16))

    pctx = contextlib.ExitStack()
    pt1s = [pctx.enter_context(nc.psum_tensor(f"pt1_{i}", [128, 1024], BF16))
            for i in range(2)]

    lctx = contextlib.ExitStack()
    xst = [lctx.enter_context(nc.sbuf_tensor(f"xst{i}", [128, D], F32))
           for i in range(2)]
    lnt = [lctx.enter_context(nc.sbuf_tensor(f"lnt{i}", [128, D], BF16))
           for i in range(2)]
    lscr = [lctx.enter_context(nc.sbuf_tensor(f"lscr{i}", [128, D], F32))
            for i in range(2)]

    ln_state = {"ap": [ZERO, ZERO], "ev": [ZERO, ZERO], "ln": [ZERO, ZERO]}

    def ln_phase(src_hbm, chunk_list, dst_fm):
        last_ap = ln_state["ap"]
        last_ev = ln_state["ev"]
        last_ln = ln_state["ln"]
        lastT = ZERO
        tok_done = ZERO
        for tc in chunk_list:
            b = tc % 2
            o = 7 * b
            X = xst[b]
            src = src_hbm[tc * 128:(tc + 1) * 128, :]
            tl = DMA("sync", X[:, :], src, [last_ap[b]])
            tsum = DV(lambda e, X=X, o=o: e.reduce_sum(
                sm[:, o + 0:o + 1], X[:, :], AX), [tl])
            tsq = DV(lambda e, X=X, S=lscr[b]: e.scalar_tensor_tensor(
                S[:, :], X[:, :], 1.0, X[:, :], ALU.mult, ALU.mult))
            tssq = DV(lambda e, o=o, S=lscr[b]: e.reduce_sum(
                sm[:, o + 1:o + 2], S[:, :], AX), [tsq])
            FENCE("vector")
            tmean = DV(lambda e, o=o: e.tensor_scalar_mul(
                sm[:, o + 2:o + 3], sm[:, o + 0:o + 1], 1.0 / D))
            tmsq = DV(lambda e, o=o: e.scalar_tensor_tensor(
                sm[:, o + 3:o + 4], sm[:, o + 0:o + 1], 1.0 / (D * D),
                sm[:, o + 0:o + 1], ALU.mult, ALU.mult))
            FENCE("vector")
            tvar = DV(lambda e, o=o: e.scalar_tensor_tensor(
                sm[:, o + 4:o + 5], sm[:, o + 1:o + 2], 1.0 / D,
                sm[:, o + 3:o + 4], ALU.mult, ALU.subtract),
                [last_ln[b]])
            fv3 = FENCE("vector")
            tln = AC(lambda e, o=o: e.activation(
                sm[:, o + 5:o + 6], sm[:, o + 4:o + 5], AF.Ln, bias=eps128[:, 0:1]),
                [fv3])
            FENCE("scalar")
            trs = AC(lambda e, o=o: e.activation(
                sm[:, o + 6:o + 7], sm[:, o + 5:o + 6], AF.Exp, scale=-0.5),
                [last_ap[b]])
            fa2 = FENCE("scalar")
            tap = DV(lambda e, X=X, L=lnt[b], o=o: e.tensor_scalar(
                L[:, :], X[:, :], sm[:, o + 2:o + 3], sm[:, o + 6:o + 7],
                ALU.subtract, ALU.mult), [fa2, last_ev[b]], big=True)
            last_ap[b] = tap
            last_ln[b] = tln
            PT1 = pt1s[b]
            for kc in range(8):
                lastT = PE(lambda e, L=lnt[b], kc=kc, PT1=PT1: e.transpose(
                    PT1[:, kc * 128:(kc + 1) * 128],
                    L[:, kc * 128:(kc + 1) * 128], ident_sb[:, :]),
                    [tap, tok_ident, last_ev[b]])
            dst = dst_fm[:, :, tc * 128:(tc + 1) * 128]
            src3 = PT1.ap().rearrange("p (c q) -> p c q", q=128)
            tok_done = DV(lambda e, dst=dst, src3=src3: e.tensor_copy(dst, src3),
                          [lastT], big=True)
            last_ev[b] = tok_done
        return tok_done

    tok_lnseq = ln_phase(x_seq, list(range(32)) + [0, 1, 2, 3, 4, 5], ln1_seq)
    tok_lnown = ln_phase(x_own, list(range(16)) + [0, 1, 2, 3], ln1_own)
    lctx.close()
    pctx.close()
    p.barrier()

    actx = contextlib.ExitStack()
    asb = lambda *a: actx.enter_context(nc.sbuf_tensor(*a))
    q_g = asb("q_g", [128, 4, T_OWN], BF16)
    k_g = asb("k_g", [128, 4, T_SEQ], BF16)
    v_g = asb("v_g", [128, 32, 260], BF16)
    oc = asb("oc", [128, 2, T_OWN], BF16)
    a2b = [asb(f"a2_{i}", [128, 1024], BF16) for i in range(2)]
    mkb = asb("mkb", [128, 8, 512], BF16)
    sqs = asb("sqs", [128, 512], BF16)
    wqs = asb("wqs", [128, 8, 256], BF16)
    wks = asb("wks", [128, 8, 256], BF16)
    wvs = asb("wvs", [128, 8, 256], BF16)
    rrow = asb("rrow", [1, 512], F32)
    rbf = asb("rbf", [1, 512], BF16)
    rbc = asb("rbc", [64, 512], BF16)
    PA = actx.enter_context(nc.psum_tensor("PA", [128, 1024], F32))
    PB = actx.enter_context(nc.psum_tensor("PB", [128, 1024], F32))
    PC = actx.enter_context(nc.psum_tensor("PC", [128, 1024], F32))
    PD = actx.enter_context(nc.psum_tensor("PD", [128, 512], F32))

    tok_vones = DV(lambda e: e.memset(
        v_g[:, :, :].rearrange("p t (h f) -> p t h f", h=4)[:, :, :, 64:65], 1.0))

    prev_attn_pe = ZERO
    last_exp = {0: ZERO, 1: ZERO}
    last_norm = {0: ZERO, 1: ZERO}
    last_a2 = {0: ZERO, 1: ZERO}
    prev_spill = ZERO
    prev_wdma = ZERO
    prev_mask_use = ZERO

    for g in range(4):
        fg = slice(256 * g, 256 * (g + 1))
        twq = DMA("gpsimd", wqs[:, :, :],
                  wq_t.rearrange("(kc p) n -> p kc n", p=128)[:, :, fg],
                  [prev_attn_pe])
        twk = DMA("gpsimd", wks[:, :, :],
                  wk_t.rearrange("(kc p) n -> p kc n", p=128)[:, :, fg],
                  [prev_attn_pe])
        twv = DMA("gpsimd", wvs[:, :, :],
                  wv_t.rearrange("(kc p) n -> p kc n", p=128)[:, :, fg],
                  [prev_attn_pe])
        tko = ZERO
        for hl in range(4):
            tko = DV(lambda e, hl=hl: e.memset(k_g[64:65, hl, :], 1.0),
                     [prev_attn_pe])

        last_kev = {}
        last_sq = ZERO
        for hp in range(2):
            for ts in range(8):
                par = ts % 2
                pq = PA[:, 512 * par:512 * (par + 1)]
                w0 = [twk, tok_lnseq, last_kev.get(par, ZERO), last_sq]
                for kc in range(8):
                    l = wks[:, kc, hp * 128:(hp + 1) * 128]
                    r = ln1_seq[:, kc, ts * 512:(ts + 1) * 512]
                    tkmm = MM(pq, l, r, kc == 0, kc == 7, w0 if kc == 0 else ())
                ksl0 = k_g[0:64, 2 * hp, ts * 512:(ts + 1) * 512]
                ksl1 = k_g[0:64, 2 * hp + 1, ts * 512:(ts + 1) * 512]
                te0 = AC(lambda e, o=ksl0, i=pq[0:64, :]: e.activation(
                    o, i, AF.Copy), [tkmm, prev_attn_pe], big=True)
                te1 = AC(lambda e, o=ksl1, i=pq[64:128, :]: e.activation(
                    o, i, AF.Copy), big=True)
                last_kev[par] = te1
                ts0 = DV(lambda e, i=ksl0: e.scalar_tensor_tensor(
                    sqs[0:64, :], i, 1.0, i, ALU.mult, ALU.mult),
                    [te0, last_sq], big=True)
                ts1 = DV(lambda e, i=ksl1: e.scalar_tensor_tensor(
                    sqs[64:128, :], i, 1.0, i, ALU.mult, ALU.mult), [te1],
                    big=True)
                tqm = MM(PD[0:1, 0:512], ones128[:, :], sqs[:, :], True, True,
                         [ts1, tok_o128])
                col = 16 + hp
                if ts == 0:
                    last_sq = DV(lambda e, col=col: e.reduce_max(
                        sm[0:1, col:col + 1], PD[0:1, 0:512], AX), [tqm])
                else:
                    tr = DV(lambda e: e.reduce_max(
                        sm[0:1, 18:19], PD[0:1, 0:512], AX), [tqm])
                    JD()
                    last_sq = DV(lambda e, col=col: e.tensor_max(
                        sm[0:1, col:col + 1], sm[0:1, 18:19],
                        sm[0:1, col:col + 1]), [tr])
        JD()
        tnb = [DV(lambda e, hp=hp: e.tensor_scalar(
            sm[0:1, 20 + hp:21 + hp], sm[0:1, 16 + hp:17 + hp], -0.5, -1.0,
            ALU.mult, ALU.add), [last_sq]) for hp in range(2)]

        last_qev = {}
        last_qsq = ZERO
        tok_q_done = ZERO
        for hp in range(2):
            for ts in range(4):
                par = ts % 2
                pq = PB[:, 512 * par:512 * (par + 1)]
                w0 = [twq, tok_lnown, last_qev.get(par, ZERO), last_qsq]
                for kc in range(8):
                    l = wqs[:, kc, hp * 128:(hp + 1) * 128]
                    r = ln1_own[:, kc, ts * 512:(ts + 1) * 512]
                    tqmm = MM(pq, l, r, kc == 0, kc == 7, w0 if kc == 0 else ())
                qsl0 = q_g[0:64, 2 * hp, ts * 512:(ts + 1) * 512]
                qsl1 = q_g[0:64, 2 * hp + 1, ts * 512:(ts + 1) * 512]
                te0 = AC(lambda e, o=qsl0, i=pq[0:64, :]: e.activation(
                    o, i, AF.Copy, scale=0.125), [tqmm, prev_attn_pe], big=True)
                te1 = AC(lambda e, o=qsl1, i=pq[64:128, :]: e.activation(
                    o, i, AF.Copy, scale=0.125), big=True)
                last_qev[par] = te1
                ts0 = DV(lambda e, i=qsl0: e.scalar_tensor_tensor(
                    sqs[0:64, :], i, 1.0, i, ALU.mult, ALU.mult),
                    [te0, last_qsq], big=True)
                ts1 = DV(lambda e, i=qsl1: e.scalar_tensor_tensor(
                    sqs[64:128, :], i, 1.0, i, ALU.mult, ALU.mult), [te1],
                    big=True)
                tqm = MM(PD[0:1, 0:512], ones128[:, :], sqs[:, :], True, True,
                         [ts1])
                tb0 = DV(lambda e, hp=hp, ts=ts: e.tensor_scalar(
                    q_g[64:65, 2 * hp, ts * 512:(ts + 1) * 512],
                    PD[0:1, 0:512], -0.5, sm[0:1, 20 + hp:21 + hp],
                    ALU.mult, ALU.add), [tqm, tnb[hp]])
                tb1 = DV(lambda e, hp=hp, ts=ts: e.tensor_scalar(
                    q_g[64:65, 2 * hp + 1, ts * 512:(ts + 1) * 512],
                    PD[0:1, 0:512], -0.5, sm[0:1, 20 + hp:21 + hp],
                    ALU.mult, ALU.add))
                last_qsq = tb1
                tok_q_done = tb1

        if debug and g == 0:
            DMA("sync", dbg_ln[:, :, :], ln1_seq[:, :, :], [tok_lnseq])
            DMA("sync", dbg_lno[:, :, :], ln1_own[:, :, :], [tok_lnown])
            DMA("sync", dbg_k[:, :, :], k_g[:, :, :], [tok_q_done])
            DMA("sync", dbg_q[:, :, :], q_g[:, :, :], [tok_q_done])
        if debug and g == 3:
            DMA("sync", dbg_ln2[:, :, :], ln1_seq[:, :, :], [tok_q_done])

        last_vev = {}
        for tc in range(32):
            par = tc % 2
            pv = PA[:, 512 * par:512 * par + 256]
            w0 = [twv, tok_lnseq, last_vev.get(par, ZERO), last_kev.get(par, ZERO)]
            for kc in range(8):
                l = ln1_seq[:, kc, tc * 128:(tc + 1) * 128]
                r = wvs[:, kc, :]
                tvmm = MM(pv, l, r, kc == 0, kc == 7, w0 if kc == 0 else ())
            vdst = v_g[:, tc, :].rearrange("p (h f) -> p h f", h=4)[:, :, 0:64]
            vsrc = pv.rearrange("p (h f) -> p h f", h=4)
            tvev = DV(lambda e, o=vdst, i=vsrc: e.tensor_copy(o, i),
                      [tvmm, tok_vones, prev_attn_pe], big=True)
            last_vev[par] = tvev

        tok_v_done = last_vev[1]

        tok_oc_sp = prev_spill
        toks_oc_tt = []
        for slot in range(4):
            E = EPAD[slot]
            tmsk = DMA("gpsimd", mkb[:, :, :],
                       masksT[slot].rearrange("u p q -> p u q"),
                       [prev_mask_use])
            for hp in range(2):
                h0 = 2 * hp
                h1 = 2 * hp + 1
                toks_av = []
                for c in range(E):
                    sbuf_i = c % 2
                    PS = [PA, PB][sbuf_i]
                    A2 = a2b[c % 2]
                    wS = [tok_q_done, tok_v_done, last_exp[sbuf_i]]
                    tS0 = MM(PS[:, 0:512],
                             k_g[0:65, h0, c * 128:(c + 1) * 128],
                             q_g[0:65, h0, slot * 512:(slot + 1) * 512],
                             True, True, wS)
                    tS1 = MM(PS[:, 512:1024],
                             k_g[0:65, h1, c * 128:(c + 1) * 128],
                             q_g[0:65, h1, slot * 512:(slot + 1) * 512],
                             True, True)
                    texp = AC(lambda e, A2=A2, PS=PS: e.activation(
                        A2[:, :], PS[:, :], AF.Exp), [tS1, last_a2[c % 2]],
                        big=True)
                    last_exp[sbuf_i] = texp
                    tdep = texp
                    if c >= E - 8:
                        u = c - (E - 8)
                        DV(lambda e, A2=A2, u=u: e.tensor_tensor(
                            A2[:, 0:512], A2[:, 0:512], mkb[:, u, :], ALU.mult),
                            [texp, tmsk], big=True)
                        tdep = DV(lambda e, A2=A2, u=u: e.tensor_tensor(
                            A2[:, 512:1024], A2[:, 512:1024], mkb[:, u, :],
                            ALU.mult), big=True)
                    if debug and g == 0 and slot == 3 and hp == 0 and c == 24:
                        DMA("sync", dbg_a[:, :], A2[:, :], [tdep])
                    wav = [tdep] + ([last_norm[0], last_norm[1]] if c == 0 else [])
                    tA0 = MM(PC[0:65, 0:512],
                             v_g[:, c, h0 * 65:h0 * 65 + 65],
                             A2[:, 0:512], c == 0, c == E - 1, wav)
                    tA1 = MM(PC[0:65, 512:1024],
                             v_g[:, c, h1 * 65:h1 * 65 + 65],
                             A2[:, 512:1024], c == 0, c == E - 1)
                    last_a2[c % 2] = tA1
                    toks_av.append(tA1)
                prev_mask_use = tdep if E >= 8 else prev_mask_use
                for hh in range(2):
                    off = 512 * hh
                    trc = DV(lambda e, off=off: e.reciprocal(
                        rrow[:, :], PC[64:65, off:off + 512]),
                        [toks_av[-1], last_norm[hh]])
                    JD()
                    trb = DV(lambda e: e.tensor_copy(rbf[:, :], rrow[:, :]))
                    trm = MM(PD[0:64, 0:512], ones_c1[0:1, 0:64], rbf[0:1, :],
                             True, True, [trb])
                    tre = AC(lambda e: e.activation(rbc[:, :], PD[0:64, 0:512],
                                                    AF.Copy), [trm], big=True)
                    hloc = 2 * hp + hh
                    dst = oc[64 * hh:64 * hh + 64, hp, slot * 512:(slot + 1) * 512]
                    tno = DV(lambda e, dst=dst, off=off: e.tensor_tensor(
                        dst, PC[0:64, off:off + 512], rbc[:, :], ALU.mult),
                        [tre, tok_oc_sp], big=True)
                    last_norm[hh] = tno
                    toks_oc_tt.append(tno)
        dsto = att_out.rearrange("(c p) q -> p c q", p=128)[:, 2 * g:2 * g + 2, :]
        tok_oc_sp = DMA("sync", dsto, oc[:, :, :], [toks_oc_tt[-1]])
        prev_spill = tok_oc_sp
        prev_attn_pe = toks_av[-1]
        p.barrier()

    tok_att_done = prev_spill
    actx.close()
    lnctx.close()
    p.barrier()

    midctx = contextlib.ExitStack()
    t_fm = midctx.enter_context(nc.sbuf_tensor("t_fm", [128, 8, T_OWN], BF16))

    wctx = contextlib.ExitStack()
    wsb = lambda *a: wctx.enter_context(nc.sbuf_tensor(*a))
    ns_row = wsb("ns_row", [1, D], BF16)
    ns_bc = wsb("ns_bc", [128, D], F32)
    ao_sb = wsb("ao_sb", [128, 8, T_OWN], BF16)
    wout_sb = wsb("wout_sb", [128, 8, D], BF16)
    proj = [wsb(f"proj{i}", [128, D], F32) for i in range(2)]
    wscr = [wsb(f"wscr{i}", [128, D], F32) for i in range(2)]
    xost = [wsb(f"xost{i}", [128, D], F32) for i in range(2)]
    x2st = [wsb(f"x2st{i}", [128, D], F32) for i in range(2)]
    tsc = [wsb(f"tsc{i}", [128, D], F32) for i in range(2)]
    whist = wsb("whist", [128, 16, 8], F32) if debug else None
    tbfs = [wsb(f"tbf{i}", [128, D], BF16) for i in range(2)]
    PP = [wctx.enter_context(nc.psum_tensor(f"PP{i}", [128, 1024], F32))
          for i in range(2)]
    PT = wctx.enter_context(nc.psum_tensor("PT", [128, 1024], BF16))

    tok_ao = DMA("sync", ao_sb[:, :, :],
                 att_out.rearrange("(c p) q -> p c q", p=128), [tok_att_done])
    tok_wout = DMA("gpsimd", wout_sb[:, :, :],
                   wout_t.rearrange("(kc p) n -> p kc n", p=128))
    tok_nsr = DMA("gpsimd", ns_row[:, :], ns[:, :])
    tn1 = MM(PP[0][:, 0:512], ones_c1[:, :], ns_row[:, 0:512], True, True,
             [tok_nsr, tok_o1, tok_att_done])
    tn2 = MM(PP[0][:, 512:1024], ones_c1[:, :], ns_row[:, 512:1024], True, True)
    tok_nsbc = DV(lambda e: e.tensor_copy(ns_bc[:, :], PP[0][:, :]), [tn2])

    last_copy = [tok_nsbc, ZERO]
    last_x2dma = [ZERO, ZERO]
    last_xo = [ZERO, ZERO]
    last_tev = ZERO
    last_chain = [dict(), dict()]
    tok_x2_spills = {}
    for tc in list(range(16)) + [0, 1]:
        b = tc % 2
        o = 28 * b
        pp = PP[b]
        lc = last_chain[b]
        w0 = [tok_ao, tok_wout, last_copy[b]]
        for kc in range(8):
            l = ao_sb[:, kc, tc * 128:(tc + 1) * 128]
            MM(pp[:, 0:512], l, wout_sb[:, kc, 0:512], kc == 0, False,
               w0 if kc == 0 else ())
            MM(pp[:, 512:1024], l, wout_sb[:, kc, 512:1024], kc == 0, False)
        MM(pp[:, 0:512], ones_c1[0:1, 0:128], bout_sb[0:1, 0:512], False, True,
           [tok_bout])
        tpj = MM(pp[:, 512:1024], ones_c1[0:1, 0:128], bout_sb[0:1, 512:1024],
                 False, True)
        P = proj[b]
        tcp = DV(lambda e, P=P, pp=pp: e.tensor_copy(P[:, :], pp[:, :]),
                 [tpj, lc.get("ap", ZERO)], big=True)
        if debug and tc == 4:
            DMA("sync", dbg_pj[:, :], P[:, :], [tcp])
        last_copy[b] = tcp
        tsum = DV(lambda e, P=P, o=o: e.reduce_sum(sm[:, o:o + 1], P[:, :], AX))
        tsq = DV(lambda e, P=P, S=wscr[b]: e.scalar_tensor_tensor(
            S[:, :], P[:, :], 1.0, P[:, :], ALU.mult, ALU.mult))
        tssq = DV(lambda e, o=o, S=wscr[b]: e.reduce_sum(
            sm[:, o + 1:o + 2], S[:, :], AX), [tsq])
        FENCE("vector")
        tmean = DV(lambda e, o=o: e.tensor_scalar_mul(
            sm[:, o + 2:o + 3], sm[:, o + 0:o + 1], 1.0 / D))
        tmsq = DV(lambda e, o=o: e.scalar_tensor_tensor(
            sm[:, o + 3:o + 4], sm[:, o + 0:o + 1], 1.0 / (D * D),
            sm[:, o + 0:o + 1], ALU.mult, ALU.mult))
        FENCE("vector")
        tvar = DV(lambda e, o=o: e.scalar_tensor_tensor(
            sm[:, o + 4:o + 5], sm[:, o + 1:o + 2], 1.0 / D, sm[:, o + 3:o + 4],
            ALU.mult, ALU.subtract), [lc.get("ln", ZERO)])
        fv3 = FENCE("vector")
        tln = AC(lambda e, o=o: e.activation(
            sm[:, o + 5:o + 6], sm[:, o + 4:o + 5], AF.Ln, bias=eps128[:, 0:1]),
            [fv3])
        FENCE("scalar")
        trs = AC(lambda e, o=o: e.activation(
            sm[:, o + 6:o + 7], sm[:, o + 5:o + 6], AF.Exp, scale=-0.5),
            [lc.get("ap", ZERO)])
        fa2 = FENCE("scalar")
        txo = DMA("sync", xost[b][:, :], x_own[tc * 128:(tc + 1) * 128, :],
                  [last_xo[b]])
        tap = DV(lambda e, P=P, S=wscr[b], o=o: e.tensor_scalar(
            S[:, :], P[:, :], sm[:, o + 2:o + 3], sm[:, o + 6:o + 7],
            ALU.subtract, ALU.mult), [fa2], big=True)
        tx2 = DV(lambda e, S=wscr[b], X2=x2st[b], XO=xost[b]: e.tensor_add(
            X2[:, :], S[:, :], XO[:, :]), [txo, last_x2dma[b]], big=True)
        lc["ap"] = tap
        if debug and tc in (4, 5):
            for ci, cc in enumerate([0, 1, 2, 3, 4, 5, 6]):
                DV(lambda e, tc=tc, ci=ci, cc=cc, o=o: e.tensor_copy(
                    whist[:, tc, ci:ci + 1], sm[:, o + cc:o + cc + 1]), [tt0])
            DMA("sync", dbg_ao[:, :, :], ao_sb[:, :, :], [tok_ao])
        tx2d = DMA("sync", x2s[tc * 128:(tc + 1) * 128, :], x2st[b][:, :], [tx2])
        tok_x2_spills[tc] = tx2d
        last_x2dma[b] = tx2d
        last_xo[b] = tx2
        t2sum = DV(lambda e, X2=x2st[b], o=o: e.reduce_sum(
            sm[:, o + 7:o + 8], X2[:, :], AX), [tx2])
        t2sq = DV(lambda e, X2=x2st[b], S=wscr[b]: e.scalar_tensor_tensor(
            S[:, :], X2[:, :], 1.0, X2[:, :], ALU.mult, ALU.mult))
        t2ssq = DV(lambda e, o=o, S=wscr[b]: e.reduce_sum(
            sm[:, o + 8:o + 9], S[:, :], AX), [t2sq])
        FENCE("vector")
        t2mean = DV(lambda e, o=o, tc=tc: e.tensor_scalar_mul(
            ln2st[:, tc, 0:1], sm[:, o + 7:o + 8], 1.0 / D))
        t2msq = DV(lambda e, o=o: e.scalar_tensor_tensor(
            sm[:, o + 9:o + 10], sm[:, o + 7:o + 8], 1.0 / (D * D),
            sm[:, o + 7:o + 8], ALU.mult, ALU.mult))
        FENCE("vector")
        t2var = DV(lambda e, o=o: e.scalar_tensor_tensor(
            sm[:, o + 10:o + 11], sm[:, o + 8:o + 9], 1.0 / D,
            sm[:, o + 9:o + 10], ALU.mult, ALU.subtract), [lc.get("c2", ZERO)])
        fv3 = FENCE("vector")
        t2ln = AC(lambda e, o=o: e.activation(
            sm[:, o + 11:o + 12], sm[:, o + 10:o + 11], AF.Ln,
            bias=eps128[:, 0:1]), [fv3])
        t2lc = AC(lambda e, o=o: e.activation(
            sm[:, o + 15:o + 16], sm[:, o + 10:o + 11], AF.Ln,
            scale=1.0 + EPS, bias=epsq128[:, 0:1]))
        FENCE("scalar")
        t2rs = AC(lambda e, o=o, tc=tc: e.activation(
            ln2st[:, tc, 1:2], sm[:, o + 11:o + 12], AF.Exp, scale=-0.5))
        tcomb = AC(lambda e, o=o: e.activation(
            sm[:, o + 17:o + 18], sm[:, o + 15:o + 16], AF.Exp, scale=-0.5),
            [lc.get("tt0", ZERO)])
        fa2 = FENCE("scalar")
        lc["ln"] = t2lc
        tt0 = DV(lambda e, X2=x2st[b], T=tsc[b], o=o, tc=tc: e.tensor_scalar(
            T[:, :], X2[:, :], ln2st[:, tc, 0:1], sm[:, o + 17:o + 18],
            ALU.subtract, ALU.mult), [fa2], big=True)
        lc["tt0"] = tt0
        lc["c2"] = tcomb
        ttb = DV(lambda e, T=tsc[b], TB=tbfs[b]: e.tensor_tensor(
            TB[:, :], T[:, :], ns_bc[:, :], ALU.mult), [tok_nsbc, last_tev],
            big=True)
        for kc in range(8):
            lastT = PE(lambda e, TB=tbfs[b], kc=kc: e.transpose(
                PT[:, kc * 128:(kc + 1) * 128],
                TB[:, kc * 128:(kc + 1) * 128], ident_sb[:, :]),
                [ttb, last_tev])
        dst = t_fm[:, :, tc * 128:(tc + 1) * 128]
        srcT = PT.ap().rearrange("p (c q) -> p c q", q=128)
        last_tev = DV(lambda e, dst=dst, srcT=srcT: e.tensor_copy(dst, srcT),
                      [lastT], big=True)
        lc["cb"] = last_tev

    if debug:
        DMA("sync", dbg_ws[:, :], whist[:, 4:6, :].rearrange("p a c -> p (a c)"),
            [last_tev])
        DMA("sync", dbg_t[:, :, :], t_fm[:, :, :], [last_tev])
    tok_t_done = last_tev
    wctx.close()
    p.barrier()

    gctx = contextlib.ExitStack()
    PG = gctx.enter_context(nc.psum_tensor("PG", [128, 512], F32))
    inds = gctx.enter_context(nc.sbuf_tensor("inds", [128, NE], F32))
    ind2s = gctx.enter_context(nc.sbuf_tensor("ind2s", [128, NE], F32))
    msks = gctx.enter_context(nc.sbuf_tensor("msks", [128, NE], F32))
    wtmp = gctx.enter_context(nc.sbuf_tensor("wtmp", [128, NE], F32))

    last_g = ZERO
    for tc in list(range(16)) + [0, 1]:
        o = 24 + 8 * (tc % 2)
        w0 = [tok_t_done, tok_gw, last_g]
        for kc in range(8):
            MM(PG[:, 0:NE], t_fm[:, kc, tc * 128:(tc + 1) * 128],
               gw_sb[:, kc, :], kc == 0, False, w0 if kc == 0 else ())
        tgm = MM(PG[:, 0:NE], ones_c1[0:1, 0:128], gb_sb[0:1, :], False, True,
                 [tok_gb])
        tm1 = DV(lambda e, o=o: e.reduce_max(sm[:, o:o + 1], PG[:, 0:NE], AX),
                 [tgm, last_g])
        fv1 = FENCE("vector")
        tind = DV(lambda e, o=o: e.tensor_scalar(
            inds[:, :], PG[:, 0:NE], sm[:, o:o + 1], None, ALU.is_ge))
        tnm1 = DV(lambda e, o=o: e.tensor_scalar_mul(
            sm[:, o + 2:o + 3], sm[:, o:o + 1], -1.0))
        fv2 = FENCE("vector")
        tmsk = DV(lambda e: e.scalar_tensor_tensor(
            msks[:, :], inds[:, :], -30000.0, PG[:, 0:NE], ALU.mult, ALU.add))
        fv3 = FENCE("vector")
        tm2 = DV(lambda e, o=o: e.reduce_max(sm[:, o + 1:o + 2], msks[:, :], AX))
        fv4 = FENCE("vector")
        tz = AC(lambda e, o=o: e.activation(
            sm[:, o + 3:o + 4], sm[:, o + 1:o + 2], AF.Exp,
            bias=sm[:, o + 2:o + 3]), [fv4])
        tind2 = DV(lambda e, o=o: e.tensor_scalar(
            ind2s[:, :], msks[:, :], sm[:, o + 1:o + 2], None, ALU.is_ge))
        fa1 = FENCE("scalar")
        tzp = DV(lambda e, o=o: e.tensor_scalar_add(
            sm[:, o + 4:o + 5], sm[:, o + 3:o + 4], 1.0), [fa1])
        fv5 = FENCE("vector")
        tp1 = DV(lambda e, o=o: e.reciprocal(sm[:, o + 5:o + 6], sm[:, o + 4:o + 5]))
        fv6 = FENCE("vector")
        tp2 = DV(lambda e, o=o: e.tensor_mul(
            sm[:, o + 6:o + 7], sm[:, o + 5:o + 6], sm[:, o + 3:o + 4]))
        twt = DV(lambda e, o=o: e.tensor_scalar(
            wtmp[:, :], inds[:, :], sm[:, o + 5:o + 6], None, ALU.mult))
        fv7 = FENCE("vector")
        last_g = DV(lambda e, o=o, tc=tc: e.scalar_tensor_tensor(
            w_st[:, tc, :], ind2s[:, :], sm[:, o + 6:o + 7], wtmp[:, :],
            ALU.mult, ALU.add), [fv7])
    tok_gate = last_g
    if debug:
        fvg = FENCE("vector")
        DMA("sync", dbg_wst[:, :, :], w_st[:, :, :], [fvg])
    gctx.close()
    p.barrier()

    mctx = contextlib.ExitStack()
    msb = lambda *a: mctx.enter_context(nc.sbuf_tensor(*a))
    act_st = msb("act_st", [128, 16, T_OWN], BF16)
    w2_sb = msb("w2_sb", [128, 16, D], BF16)
    w1_sb = [msb(f"w1_sb{i}", [128, 8, 512], BF16) for i in range(2)]
    wbf = msb("wbf", [128, NE], BF16)
    wTs = msb("wTs", [NE, 128], BF16)
    ytc = [msb(f"ytc{i}", [128, D], F32) for i in range(2)]
    gsc = [msb(f"gsc{i}", [128, 512], F32) for i in range(2)]
    ssb = [msb(f"ssb{i}", [128, 512], BF16) for i in range(2)]
    lsb = [msb(f"lsb{i}", [128, 512], BF16) for i in range(2)]
    qsb = [msb(f"qsb{i}", [128, 512], BF16) for i in range(2)]
    PYC = [mctx.enter_context(nc.psum_tensor(f"PYC{i}", [128, 1024], F32))
           for i in range(2)]

    last_yi = [ZERO, ZERO]
    tok_init = []
    FENCE("vector")
    with nc.psum_tensor("PTW", [128, 128], BF16) as PTW:
        last_tr = ZERO
        for tc in range(16):
            b = tc % 2
            twb = DV(lambda e, tc=tc: e.tensor_copy(wbf[:, :], w_st[:, tc, :]),
                     [tok_gate if tc == 0 else tok_init[-1]])
            fvb = FENCE("vector")
            ttr = PE(lambda e: e.transpose(PTW[0:NE, 0:128], wbf[:, :],
                                           ident_sb[:, :]), [fvb, last_tr])
            twt = DV(lambda e: e.tensor_copy(wTs[:, :], PTW[0:NE, 0:128]), [ttr])
            fvc = FENCE("vector")
            last_tr = twt
            m1 = MM(PYC[b][:, 0:512], wTs[0:NE, :], b2_sb[0:NE, 0:512],
                    True, True, [fvc, tok_b2, last_yi[b]])
            m2 = MM(PYC[b][:, 512:1024], wTs[0:NE, :], b2_sb[0:NE, 512:1024],
                    True, True)
            tyc = DV(lambda e, b=b, pp=PYC[b]: e.tensor_copy(
                ytc[b][:, :], pp[:, :]), [m2], big=True)
            last_yi[b] = tyc
            tok_init.append(DMA("gpsimd", y_dram[tc * 128:(tc + 1) * 128, :],
                                ytc[b][:, :], [tyc]))
    tok_init_all = tok_init[-1]
    PHA = mctx.enter_context(nc.psum_tensor("PHA", [128, 1024], F32))
    PHB = mctx.enter_context(nc.psum_tensor("PHB", [128, 1024], F32))

    last_w1 = [ZERO, ZERO]
    last_phB_pe = ZERO
    accum_tc = {tc: tok_init[tc] for tc in range(16)}
    last_ev_g = [ZERO, ZERO]
    last_ev_l = [ZERO, ZERO]
    last_act_tt = [ZERO, ZERO]
    last_ytc = [tok_init[14], tok_init[15]]
    tok_phB_done = ZERO
    tok_accum_last = tok_init_all
    for e_i in range(NE):
        w1_toks = {}
        src4 = w1[e_i].rearrange("(kc p) n -> p kc n", p=128)

        def issue_w1(ipp, guard):
            buf = ipp % 2
            sg = src4[:, :, ipp * 256:(ipp + 1) * 256]
            sl = src4[:, :, 2048 + ipp * 256:2048 + (ipp + 1) * 256]
            DMA("gpsimd", w1_sb[buf][:, :, 0:256], sg, [guard])
            w1_toks[ipp] = DMA("gpsimd", w1_sb[buf][:, :, 256:512], sl)

        issue_w1(0, last_w1[0])
        issue_w1(1, last_w1[1])
        tok_w2 = DMA("gpsimd", w2_sb[:, :, :],
                     w2[e_i].rearrange("(ic p) n -> p ic n", p=128),
                     [last_phB_pe])

        uid = 0
        for ip in range(16):
            ipp, j = ip // 2, ip % 2
            buf = ipp % 2
            W1 = w1_sb[buf]
            for tt in range(4):
                ub = uid % 2
                pg = PHA[:, 512 * (tt % 2):512 * (tt % 2 + 1)]
                pl = PHB[:, 512 * (tt % 2):512 * (tt % 2 + 1)]
                w0 = [w1_toks[ipp], tok_t_done,
                      last_ev_g[tt % 2], last_ev_l[tt % 2]]
                if e_i == 0 and ip == 0 and tt == 0:
                    w0.append(tok_gate)
                    w0.append(tok_init_all)
                if e_i > 0:
                    w0.append(tok_phB_done)
                for kc in range(8):
                    lg = W1[:, kc, j * 128:(j + 1) * 128]
                    rr = t_fm[:, kc, tt * 512:(tt + 1) * 512]
                    MM(pg, lg, rr, kc == 0, kc == 7, w0 if kc == 0 else ())
                for kc in range(8):
                    ll = W1[:, kc, 256 + j * 128:256 + (j + 1) * 128]
                    rr = t_fm[:, kc, tt * 512:(tt + 1) * 512]
                    tml = MM(pl, ll, rr, kc == 0, kc == 7)
                bg = b1c_sb[:, e_i, ip:ip + 1]
                bl = b1c_sb[:, e_i, 16 + ip:17 + ip]
                tg0 = DV(lambda e, pg=pg, G=gsc[ub], bg=bg: e.tensor_scalar(
                    G[:, :], pg, bg, 7.0, ALU.add, ALU.min),
                    [tml, tok_b1p, last_act_tt[ub]], big=True)
                last_ev_g[tt % 2] = tg0
                tgs = AC(lambda e, G=gsc[ub], S=ssb[ub]: e.activation(
                    S[:, :], G[:, :], AF.Gelu_apprx_sigmoid), [tg0], big=True)
                tl0 = AC(lambda e, pl=pl, L=lsb[ub], bl=bl: e.activation(
                    L[:, :], pl, AF.Identity, bias=bl), [tml], big=True)
                last_ev_l[tt % 2] = tl0
                tq = DV(lambda e, L=lsb[ub], Q=qsb[ub]: e.tensor_scalar(
                    Q[:, :], L[:, :], 8.0, -6.0, ALU.min, ALU.max), [tl0],
                    big=True)
                adst = act_st[:, ip, tt * 512:(tt + 1) * 512]
                ta = DV(lambda e, adst=adst, S=ssb[ub], Q=qsb[ub]:
                        e.tensor_tensor(adst, Q[:, :], S[:, :], ALU.mult),
                        [tgs, tq, tok_phB_done], big=True)
                last_act_tt[ub] = ta
                uid += 1
            if j == 1 and ipp + 2 < 8:
                issue_w1(ipp + 2, tml)
        tok_actA = last_act_tt[(uid - 1) % 2]
        last_w1 = [tok_actA, tok_actA]

        for tcm in range(16):
            b = tcm % 2
            py = PYC[b]
            w0 = [tok_w2, tok_actA, last_ytc[b]]
            for ic in range(16):
                l = act_st[:, ic, tcm * 128:(tcm + 1) * 128]
                MM(py[:, 0:512], l, w2_sb[:, ic, 0:512], ic == 0, False,
                   w0 if ic == 0 else ())
                tmB = MM(py[:, 512:1024], l, w2_sb[:, ic, 512:1024],
                         ic == 0, ic == 15)
            last_phB_pe = tmB
            wcol = w_st[:, tcm, e_i:e_i + 1]
            tst = DV(lambda e, py=py, Y=ytc[b], wcol=wcol: e.tensor_scalar(
                Y[:, :], py[:, :], wcol, None, ALU.mult), [tmB, last_ytc[b]],
                big=True)
            if tcm == 15:
                tok_phB_done = tst
            tok_ac = DMA(
                "gpsimd", y_dram[tcm * 128:(tcm + 1) * 128, :], ytc[b][:, :],
                [tst, accum_tc[tcm]], accum_op=ALU.add)
            accum_tc[tcm] = tok_ac
            last_ytc[b] = tok_ac
            tok_accum_last = tok_ac
        p.barrier()
    tok_y_done = tok_accum_last
    mctx.close()
    midctx.close()
    p.barrier()

    fctx = contextlib.ExitStack()
    fsb = lambda *a: fctx.enter_context(nc.sbuf_tensor(*a))
    fx2 = [fsb(f"fx2{i}", [128, D], F32) for i in range(2)]
    fy = [fsb(f"fy{i}", [128, D], F32) for i in range(2)]
    fz = [fsb(f"fz{i}", [128, D], F32) for i in range(2)]
    fs = [fsb(f"fs{i}", [128, D], F32) for i in range(2)]
    fo = [fsb(f"fo{i}", [128, D], F32) for i in range(2)]

    last_fin = [dict(), dict()]
    out_toks = []
    for tc in list(range(16)) + [0, 1]:
        b = tc % 2
        o = 72 + 12 * b
        lf = last_fin[b]
        tlx = DMA("sync", fx2[b][:, :], x2s[tc * 128:(tc + 1) * 128, :],
                  [tok_x2_spills[tc], lf.get("xo", ZERO)])
        tly = DMA("sync", fy[b][:, :], y_dram[tc * 128:(tc + 1) * 128, :],
                  [tok_y_done, lf.get("ffn", ZERO)])
        tz = DV(lambda e, X=fx2[b], Z=fz[b], tc=tc: e.tensor_scalar(
            Z[:, :], X[:, :], ln2st[:, tc, 0:1], ln2st[:, tc, 1:2],
            ALU.subtract, ALU.mult), [tlx, lf.get("sq", ZERO)])
        tf = DV(lambda e, Z=fz[b], Y=fy[b]: e.tensor_add(
            Z[:, :], Z[:, :], Y[:, :]), [tly], big=True)
        lf["ffn"] = tf
        tsum = DV(lambda e, Z=fz[b], o=o: e.reduce_sum(sm[:, o:o + 1], Z[:, :], AX))
        tsq = DV(lambda e, Z=fz[b], S=fs[b]: e.scalar_tensor_tensor(
            S[:, :], Z[:, :], 1.0, Z[:, :], ALU.mult, ALU.mult))
        lf["sq"] = tsq
        tssq = DV(lambda e, S=fs[b], o=o: e.reduce_sum(
            sm[:, o + 1:o + 2], S[:, :], AX), [tsq])
        FENCE("vector")
        tmean = DV(lambda e, o=o: e.tensor_scalar_mul(
            sm[:, o + 2:o + 3], sm[:, o + 0:o + 1], 1.0 / D))
        tmsq = DV(lambda e, o=o: e.scalar_tensor_tensor(
            sm[:, o + 3:o + 4], sm[:, o + 0:o + 1], 1.0 / (D * D),
            sm[:, o + 0:o + 1], ALU.mult, ALU.mult))
        FENCE("vector")
        tvar = DV(lambda e, o=o: e.scalar_tensor_tensor(
            sm[:, o + 4:o + 5], sm[:, o + 1:o + 2], 1.0 / D, sm[:, o + 3:o + 4],
            ALU.mult, ALU.subtract), [lf.get("ln", ZERO)])
        fv3 = FENCE("vector")
        tln = AC(lambda e, o=o: e.activation(
            sm[:, o + 5:o + 6], sm[:, o + 4:o + 5], AF.Ln, bias=eps128[:, 0:1]),
            [fv3])
        lf["ln"] = tln
        FENCE("scalar")
        trs = AC(lambda e, o=o: e.activation(
            sm[:, o + 6:o + 7], sm[:, o + 5:o + 6], AF.Exp, scale=-0.5),
            [lf.get("ap", ZERO)])
        fa2 = FENCE("scalar")
        tap = DV(lambda e, Z=fz[b], S=fs[b], o=o: e.tensor_scalar(
            S[:, :], Z[:, :], sm[:, o + 2:o + 3], sm[:, o + 6:o + 7],
            ALU.subtract, ALU.mult), [fa2])
        lf["ap"] = tap
        txo = DV(lambda e, S=fs[b], X=fx2[b], O=fo[b]: e.tensor_add(
            O[:, :], S[:, :], X[:, :]), [tap, lf.get("od", ZERO)], big=True)
        lf["xo"] = txo
        tod = DMA("sync", out[tc * 128:(tc + 1) * 128, :], fo[b][:, :], [txo])
        lf["od"] = tod
        out_toks.append(tod)

    p.op("sync", lambda e: e.nop(), [out_toks[-1]])
    fctx.close()

    semnums = sorted(s.num for s in sems.values())
    with nc.Block() as b0:
        @b0.gpsimd
        def _(g):
            g.dma_reset(range(semnums[0], semnums[-1] + 1))
            g.sem_clear(range(semnums[0], semnums[-1] + 1))
    nc.all_engine_barrier()
    p.emit(nc, sems)
    ctx.close()
    return nc


_NC_CACHE = {}


def _get_nc():
    if "nc" not in _NC_CACHE:
        _NC_CACHE["nc"] = build_program()
    return _NC_CACHE["nc"]


def _build_masks(pat):
    m = np.zeros((4, 8, 128, 512), dtype=np.float32)
    for k in range(4):
        cls = CLS[pat][k]
        q0 = 512 * cls
        for u in range(8):
            c = EPAD[k] - 8 + u
            kv = 128 * c + np.arange(128)[:, None]
            q = q0 + np.arange(512)[None, :]
            m[k, u] = (kv <= q).astype(np.float32)
    return m


def prep_in_maps(L0_attn_x, L0_attn_Wqkv, L0_attn_Wout, L0_attn_bout,
                 L0_ffn_norm_scale, L0_ffn_gate_w, L0_ffn_gate_b,
                 L0_ffn_mlp1_w, L0_ffn_mlp1_b, L0_ffn_mlp2_w, L0_ffn_mlp2_b):
    X = np.ascontiguousarray(np.asarray(L0_attn_x, dtype=np.float32))
    Wqkv = np.asarray(L0_attn_Wqkv, dtype=np.float32)
    wq_t = np.ascontiguousarray(Wqkv[0:D].T)
    wk_t = np.ascontiguousarray(Wqkv[D:2 * D].T)
    wv_t = np.ascontiguousarray(Wqkv[2 * D:3 * D].T)
    wout_t = np.ascontiguousarray(np.asarray(L0_attn_Wout, dtype=np.float32).T)
    boutv = np.asarray(L0_attn_bout, dtype=np.float32).reshape(1, D)
    nsv = np.asarray(L0_ffn_norm_scale, dtype=np.float32).reshape(1, D)
    gwv = np.ascontiguousarray(
        np.asarray(L0_ffn_gate_w, dtype=np.float32).reshape(8, 128, NE)
        .transpose(1, 0, 2))
    gbv = np.asarray(L0_ffn_gate_b, dtype=np.float32).reshape(1, NE)
    w1v = np.asarray(L0_ffn_mlp1_w, dtype=np.float32)
    w1_de = np.ascontiguousarray(
        np.concatenate([w1v[:, :, 0::2], w1v[:, :, 1::2]], axis=2))
    b1v = np.asarray(L0_ffn_mlp1_b, dtype=np.float32)
    b1_de = np.concatenate([b1v[:, 0::2], b1v[:, 1::2]], axis=1)
    b1cv = np.ascontiguousarray(
        b1_de.reshape(NE, 32, 128).transpose(0, 2, 1))
    w2v = np.ascontiguousarray(np.asarray(L0_ffn_mlp2_w, dtype=np.float32))
    b2v = np.ascontiguousarray(np.asarray(L0_ffn_mlp2_b, dtype=np.float32))
    identv = np.eye(128, dtype=np.float32)
    masks_by_pat = [_build_masks(0), _build_masks(1)]

    in_maps = []
    for c in range(8):
        pat, s = c // 4, c % 4
        xs = np.ascontiguousarray(X[s])
        xo = np.ascontiguousarray(
            np.concatenate([X[s][cls * 512:(cls + 1) * 512] for cls in CLS[pat]]))
        in_maps.append({
            "x_seq": xs, "x_own": xo,
            "wq_t": wq_t, "wk_t": wk_t, "wv_t": wv_t, "wout_t": wout_t,
            "bout": boutv, "ns": nsv, "gw": gwv, "gb": gbv,
            "w1": w1_de, "b1c": b1cv, "w2": w2v, "b2": b2v,
            "masks": masks_by_pat[pat], "ident": identv,
        })
    return in_maps


def kernel(**inputs):
    in_maps = prep_in_maps(**inputs)
    nc = _get_nc()
    res = run_bass_kernel_spmd(nc, in_maps, core_ids=list(range(8)))
    outp = np.empty((4, T_SEQ, D), dtype=np.float32)
    for c in range(8):
        pat, s = c // 4, c % 4
        oc = res.results[c]["out"]
        for k, cls in enumerate(CLS[pat]):
            outp[s, cls * 512:(cls + 1) * 512] = oc[k * 512:(k + 1) * 512]
    return outp


# version 13 (final)
# speedup vs baseline: 1.0267x; 1.0267x over previous
import contextlib
import numpy as np
import concourse.bass as bass
import concourse.mybir as mybir
from concourse.bass_utils import run_bass_kernel_spmd

F32 = mybir.dt.float32
BF16 = mybir.dt.bfloat16
AX = mybir.AxisListType.X
ALU = mybir.AluOpType
AF = mybir.ActivationFunctionType

D = 1024
T_SEQ = 4096
T_OWN = 2048
NE = 8
I2 = 4096
EPS = 1e-5
CLS = [[0, 3, 4, 7], [1, 2, 5, 6]]
EPAD = [8, 16, 24, 32]

ZERO = ("PE", 0)


class Plan:
    ENGS = ("sync", "scalar", "vector", "tensor", "gpsimd")
    ENG_OF_SEM = {"DV": "vector", "AC": "scalar"}

    def __init__(self):
        self.plan = {e: [] for e in self.ENGS}
        self.cnt = {}
        self.hwm = {}
        self.junk_fn = {}
        self.nofence = set()

    def op(self, eng, fn, waits=(), inc=None, inc_amt=1, no_fence=False):
        fenced = []
        for w in waits:
            if w is None:
                continue
            sem, tgt = w
            if tgt <= 0:
                continue
            peng = self.ENG_OF_SEM.get(sem)
            if peng is not None and peng != eng and (sem, tgt) not in self.nofence:
                if self.cnt.get(sem, 0) == tgt and sem in self.junk_fn:
                    self.cnt[sem] = self.cnt.get(sem, 0) + 1
                    self.plan[peng].append((self.junk_fn[sem], [], sem, 1))
                if self.cnt.get(sem, 0) > tgt:
                    tgt += 1
            fenced.append((sem, tgt))
        pruned = []
        for sem, tgt in fenced:
            key = (eng, sem)
            if self.hwm.get(key, 0) >= tgt:
                continue
            self.hwm[key] = tgt
            pruned.append((sem, tgt))
        tok = None
        if inc is not None:
            self.cnt[inc] = self.cnt.get(inc, 0) + inc_amt
            tok = (inc, self.cnt[inc])
            if no_fence:
                self.nofence.add(tok)
        self.plan[eng].append((fn, pruned, inc, inc_amt))
        return tok

    def barrier(self):
        for e in self.ENGS:
            self.plan[e].append(("BARRIER", None, None, None))

    def emit(self, nc, sems):
        nseg = sum(1 for it in self.plan["sync"] if it[0] == "BARRIER") + 1
        pos = {e: 0 for e in self.ENGS}
        for seg in range(nseg):
            with nc.Block() as block:
                def mk(eng_name):
                    def run(eng):
                        items = self.plan[eng_name]
                        i = pos[eng_name]
                        while i < len(items) and items[i][0] != "BARRIER":
                            fn, waits, inc, amt = items[i]
                            for sem, tgt in waits:
                                eng.wait_ge(sems[sem], tgt)
                            inst = fn(eng)
                            if inc is not None:
                                inst.then_inc(sems[inc], amt)
                            i += 1
                        pos[eng_name] = i + 1
                    return run
                block.sync(mk("sync"))
                block.scalar(mk("scalar"))
                block.vector(mk("vector"))
                block.tensor(mk("tensor"))
                block.gpsimd(mk("gpsimd"))


def build_program(debug=False):
    nc = bass.Bass("TRN2", target_bir_lowering=False)
    IKIND = "ExternalOutput" if debug else "Internal"

    x_seq = nc.dram_tensor("x_seq", [T_SEQ, D], F32, kind="ExternalInput").ap()
    x_own = nc.dram_tensor("x_own", [T_OWN, D], F32, kind="ExternalInput").ap()
    wq_t = nc.dram_tensor("wq_t", [D, D], F32, kind="ExternalInput").ap()
    wk_t = nc.dram_tensor("wk_t", [D, D], F32, kind="ExternalInput").ap()
    wv_t = nc.dram_tensor("wv_t", [D, D], F32, kind="ExternalInput").ap()
    wout_t = nc.dram_tensor("wout_t", [D, D], F32, kind="ExternalInput").ap()
    bout = nc.dram_tensor("bout", [1, D], F32, kind="ExternalInput").ap()
    ns = nc.dram_tensor("ns", [1, D], F32, kind="ExternalInput").ap()
    gw = nc.dram_tensor("gw", [128, 8, NE], F32, kind="ExternalInput").ap()
    gb = nc.dram_tensor("gb", [1, NE], F32, kind="ExternalInput").ap()
    w1 = nc.dram_tensor("w1", [NE, D, I2], F32, kind="ExternalInput").ap()
    b1c = nc.dram_tensor("b1c", [NE, 128, 32], F32, kind="ExternalInput").ap()
    w2 = nc.dram_tensor("w2", [NE, 2048, D], F32, kind="ExternalInput").ap()
    b2 = nc.dram_tensor("b2", [NE, D], F32, kind="ExternalInput").ap()
    masksT = nc.dram_tensor("masks", [4, 8, 128, 512], F32, kind="ExternalInput").ap()
    ident = nc.dram_tensor("ident", [128, 128], F32, kind="ExternalInput").ap()
    out = nc.dram_tensor("out", [T_OWN, D], F32, kind="ExternalOutput").ap()

    x2s = nc.dram_tensor("x2s", [T_OWN, D], F32, kind=IKIND).ap()
    att_out = nc.dram_tensor("att_out", [1024, T_OWN], BF16, kind=IKIND).ap()
    y_dram = nc.dram_tensor("y_dram", [T_OWN, D], F32, kind=IKIND).ap()
    if debug:
        dbg_ln = nc.dram_tensor("dbg_ln", [128, 8, T_SEQ], BF16,
                                kind="ExternalOutput").ap()
        dbg_ln2 = nc.dram_tensor("dbg_ln2", [128, 8, T_SEQ], BF16,
                                 kind="ExternalOutput").ap()
        dbg_lno = nc.dram_tensor("dbg_lno", [128, 8, T_OWN], BF16,
                                 kind="ExternalOutput").ap()
        dbg_k = nc.dram_tensor("dbg_k", [128, 4, T_SEQ], BF16,
                               kind="ExternalOutput").ap()
        dbg_q = nc.dram_tensor("dbg_q", [128, 4, T_OWN], BF16,
                               kind="ExternalOutput").ap()
        dbg_a = nc.dram_tensor("dbg_a", [128, 1024], BF16,
                               kind="ExternalOutput").ap()
        dbg_pj = nc.dram_tensor("dbg_pj", [128, 1024], F32,
                                kind="ExternalOutput").ap()
        dbg_ao = nc.dram_tensor("dbg_ao", [128, 8, T_OWN], BF16,
                                kind="ExternalOutput").ap()
        dbg_ws = nc.dram_tensor("dbg_ws", [128, 16], F32,
                                kind="ExternalOutput").ap()
        dbg_t = nc.dram_tensor("dbg_t", [128, 8, T_OWN], BF16,
                               kind="ExternalOutput").ap()
        dbg_wst = nc.dram_tensor("dbg_wst", [128, 16, NE], F32,
                                 kind="ExternalOutput").ap()

    p = Plan()
    lane_rr = {"sync": 0, "gpsimd": 0}

    def DMA(queue, o, i, waits=(), **kw):
        pre = "dS" if queue == "sync" else "dQ"
        sem = f"{pre}{lane_rr[queue] % 8}"
        lane_rr[queue] += 1
        def fn(e, o=o, i=i, kw=kw):
            return e.dma_start(out=o, in_=i, **kw)
        return p.op(queue, fn, waits, sem, 16)

    def PE(fn, waits=()):
        return p.op("tensor", fn, waits, "PE")

    def DV(fn, waits=(), big=False):
        return p.op("vector", fn, waits, "DV", no_fence=big)

    def AC(fn, waits=(), big=False):
        return p.op("scalar", fn, waits, "AC", no_fence=big)

    def MM(o, l, r, start, stop, waits=()):
        def fn(e, o=o, l=l, r=r, start=start, stop=stop):
            return e.matmul(o, l, r, start=start, stop=stop)
        return PE(fn, waits)

    ctx = contextlib.ExitStack()
    sem_names = (["PE", "AC", "DV", "FN"] + [f"dS{i}" for i in range(8)]
                 + [f"dQ{i}" for i in range(8)])
    sems = {n: ctx.enter_context(nc.semaphore(n)) for n in sem_names}
    sb = lambda *a, **k: ctx.enter_context(nc.sbuf_tensor(*a, **k))

    junkv = ctx.enter_context(nc.sbuf_tensor("junkv", [128, 320], F32))
    _jc = [0]

    def _junk_dv(e):
        _jc[0] += 1
        return e.memset(junkv[:, 0:256], float(_jc[0] % 13))

    def _junk_ac(e):
        _jc[0] += 1
        return e.activation(junkv[:, 256:320], junkv[:, 256:320], AF.Identity,
                            scale=float(_jc[0] % 11))

    p.junk_fn["DV"] = _junk_dv
    p.junk_fn["AC"] = _junk_ac
    DV(lambda e: e.memset(junkv[:, :], 0.0))

    def JD():
        return p.op("vector", p.junk_fn["DV"], (), "DV")

    def JA():
        return p.op("scalar", p.junk_fn["AC"], (), "AC")

    def FENCE(eng):
        sem = "DV" if eng == "vector" else "AC"
        jf = p.junk_fn[sem]
        tm = p.op(eng, jf, (), sem, no_fence=True)
        tf = p.op("sync", lambda e: e.nop(), [tm], "FN")
        return p.op(eng, jf, [tf], sem, no_fence=True)

    ident_sb = sb("ident_sb", [128, 128], BF16)
    ones_c1 = sb("ones_c1", [1, 128], BF16)
    ones128 = sb("ones128", [128, 1], BF16)
    bout_sb = sb("bout_sb", [1, D], BF16)
    gw_sb = sb("gw_sb", [128, 8, NE], BF16)
    gb_sb = sb("gb_sb", [1, NE], BF16)
    b1c_sb = sb("b1c_sb", [128, NE, 32], F32)
    b2_sb = sb("b2_sb", [NE, D], BF16)
    ln2st = sb("ln2st", [128, 16, 2], F32)
    w_st = sb("w_st", [128, 16, NE], F32)
    sm = sb("sm", [128, 96], F32)

    tok_ident = DMA("gpsimd", ident_sb[:, :], ident[:, :])
    tok_bout = DMA("gpsimd", bout_sb[:, :], bout[:, :])
    tok_gw = DMA("gpsimd", gw_sb[:, :, :], gw[:, :, :])
    tok_gb = DMA("gpsimd", gb_sb[:, :], gb[:, :])
    tok_b1c = DMA("gpsimd", b1c_sb[:, :, :], b1c[:, :, :])
    tok_b2 = DMA("gpsimd", b2_sb[:, :], b2[:, :])
    tok_o1 = DV(lambda e: e.memset(ones_c1[:, :], 1.0))
    tok_o128 = DV(lambda e: e.memset(ones128[:, :], 1.0))
    eps128 = sb("eps128", [128, 1], F32)
    tok_eps = DV(lambda e: e.memset(eps128[:, :], EPS))
    epsq128 = sb("epsq128", [128, 1], F32)
    tok_epsq = DV(lambda e: e.memset(epsq128[:, :], EPS * EPS))
    tok_b1p = DV(lambda e: e.tensor_scalar_add(
        b1c_sb[:, :, 16:32], b1c_sb[:, :, 16:32], 1.0), [tok_b1c])

    wrm = sb("wrm", [128, 512], F32)
    t_w = DV(lambda e: e.memset(wrm[:, :], 1.0))
    for _i in range(40):
        t_w = AC(lambda e: e.activation(wrm[:, :], wrm[:, :],
                                        AF.Gelu_apprx_sigmoid), [t_w], big=True)
    for _i in range(20):
        t_w = AC(lambda e: e.activation(wrm[:, :], wrm[:, :], AF.Ln, bias=1.0),
                 [t_w], big=True)
        t_w = AC(lambda e: e.activation(wrm[:, :], wrm[:, :], AF.Exp, scale=0.1),
                 [t_w], big=True)
    tok_warm = t_w

    p.barrier()
    lnctx = contextlib.ExitStack()
    ln1_seq = lnctx.enter_context(nc.sbuf_tensor("ln1_seq", [128, 8, T_SEQ], BF16))
    ln1_own = lnctx.enter_context(nc.sbuf_tensor("ln1_own", [128, 8, T_OWN], BF16))

    pctx = contextlib.ExitStack()
    pt1s = [pctx.enter_context(nc.psum_tensor(f"pt1_{i}", [128, 1024], BF16))
            for i in range(2)]

    lctx = contextlib.ExitStack()
    xst = [lctx.enter_context(nc.sbuf_tensor(f"xst{i}", [128, D], F32))
           for i in range(2)]
    lnt = [lctx.enter_context(nc.sbuf_tensor(f"lnt{i}", [128, D], BF16))
           for i in range(2)]
    lscr = [lctx.enter_context(nc.sbuf_tensor(f"lscr{i}", [128, D], F32))
            for i in range(2)]

    ln_state = {"ap": [ZERO, ZERO], "ev": [ZERO, ZERO], "ln": [ZERO, ZERO]}

    def ln_phase(src_hbm, chunk_list, dst_fm):
        last_ap = ln_state["ap"]
        last_ev = ln_state["ev"]
        last_ln = ln_state["ln"]
        lastT = ZERO
        tok_done = ZERO
        for tc in chunk_list:
            b = tc % 2
            o = 7 * b
            X = xst[b]
            src = src_hbm[tc * 128:(tc + 1) * 128, :]
            tl = DMA("sync", X[:, :], src, [last_ap[b]])
            tsum = DV(lambda e, X=X, o=o: e.reduce_sum(
                sm[:, o + 0:o + 1], X[:, :], AX), [tl])
            tsq = DV(lambda e, X=X, S=lscr[b]: e.scalar_tensor_tensor(
                S[:, :], X[:, :], 1.0, X[:, :], ALU.mult, ALU.mult))
            tssq = DV(lambda e, o=o, S=lscr[b]: e.reduce_sum(
                sm[:, o + 1:o + 2], S[:, :], AX), [tsq])
            FENCE("vector")
            tmean = DV(lambda e, o=o: e.tensor_scalar_mul(
                sm[:, o + 2:o + 3], sm[:, o + 0:o + 1], 1.0 / D))
            tmsq = DV(lambda e, o=o: e.scalar_tensor_tensor(
                sm[:, o + 3:o + 4], sm[:, o + 0:o + 1], 1.0 / (D * D),
                sm[:, o + 0:o + 1], ALU.mult, ALU.mult))
            FENCE("vector")
            tvar = DV(lambda e, o=o: e.scalar_tensor_tensor(
                sm[:, o + 4:o + 5], sm[:, o + 1:o + 2], 1.0 / D,
                sm[:, o + 3:o + 4], ALU.mult, ALU.subtract),
                [last_ln[b]])
            fv3 = FENCE("vector")
            tln = AC(lambda e, o=o: e.activation(
                sm[:, o + 5:o + 6], sm[:, o + 4:o + 5], AF.Ln, bias=eps128[:, 0:1]),
                [fv3])
            FENCE("scalar")
            trs = AC(lambda e, o=o: e.activation(
                sm[:, o + 6:o + 7], sm[:, o + 5:o + 6], AF.Exp, scale=-0.5),
                [last_ap[b]])
            fa2 = FENCE("scalar")
            tap = DV(lambda e, X=X, L=lnt[b], o=o: e.tensor_scalar(
                L[:, :], X[:, :], sm[:, o + 2:o + 3], sm[:, o + 6:o + 7],
                ALU.subtract, ALU.mult), [fa2, last_ev[b]], big=True)
            last_ap[b] = tap
            last_ln[b] = tln
            PT1 = pt1s[b]
            for kc in range(8):
                lastT = PE(lambda e, L=lnt[b], kc=kc, PT1=PT1: e.transpose(
                    PT1[:, kc * 128:(kc + 1) * 128],
                    L[:, kc * 128:(kc + 1) * 128], ident_sb[:, :]),
                    [tap, tok_ident, last_ev[b]])
            dst = dst_fm[:, :, tc * 128:(tc + 1) * 128]
            src3 = PT1.ap().rearrange("p (c q) -> p c q", q=128)
            tok_done = DV(lambda e, dst=dst, src3=src3: e.tensor_copy(dst, src3),
                          [lastT], big=True)
            last_ev[b] = tok_done
        return tok_done

    tok_lnseq = ln_phase(x_seq, list(range(32)), ln1_seq)
    tok_lnown = ln_phase(x_own, list(range(16)), ln1_own)
    lctx.close()
    pctx.close()
    p.barrier()

    actx = contextlib.ExitStack()
    asb = lambda *a: actx.enter_context(nc.sbuf_tensor(*a))
    q_g = asb("q_g", [128, 4, T_OWN], BF16)
    k_g = asb("k_g", [128, 4, T_SEQ], BF16)
    v_g = asb("v_g", [128, 32, 260], BF16)
    oc = asb("oc", [128, 2, T_OWN], BF16)
    a2b = [asb(f"a2_{i}", [128, 1024], BF16) for i in range(2)]
    mkb = asb("mkb", [128, 8, 512], BF16)
    sqs = asb("sqs", [128, 512], BF16)
    wqs = asb("wqs", [128, 8, 256], BF16)
    wks = asb("wks", [128, 8, 256], BF16)
    wvs = asb("wvs", [128, 8, 256], BF16)
    rrow = asb("rrow", [1, 512], F32)
    rbf = asb("rbf", [1, 512], BF16)
    rbc = asb("rbc", [64, 512], BF16)
    PA = actx.enter_context(nc.psum_tensor("PA", [128, 1024], F32))
    PB = actx.enter_context(nc.psum_tensor("PB", [128, 1024], F32))
    PC = actx.enter_context(nc.psum_tensor("PC", [128, 1024], F32))
    PD = actx.enter_context(nc.psum_tensor("PD", [128, 512], F32))

    tok_vones = DV(lambda e: e.memset(
        v_g[:, :, :].rearrange("p t (h f) -> p t h f", h=4)[:, :, :, 64:65], 1.0))

    prev_attn_pe = ZERO
    last_exp = {0: ZERO, 1: ZERO}
    last_norm = {0: ZERO, 1: ZERO}
    last_a2 = {0: ZERO, 1: ZERO}
    prev_spill = ZERO
    prev_wdma = ZERO
    prev_mask_use = ZERO

    for g in range(4):
        fg = slice(256 * g, 256 * (g + 1))
        twq = DMA("gpsimd", wqs[:, :, :],
                  wq_t.rearrange("(kc p) n -> p kc n", p=128)[:, :, fg],
                  [prev_attn_pe])
        twk = DMA("gpsimd", wks[:, :, :],
                  wk_t.rearrange("(kc p) n -> p kc n", p=128)[:, :, fg],
                  [prev_attn_pe])
        twv = DMA("gpsimd", wvs[:, :, :],
                  wv_t.rearrange("(kc p) n -> p kc n", p=128)[:, :, fg],
                  [prev_attn_pe])
        tko = ZERO
        for hl in range(4):
            tko = DV(lambda e, hl=hl: e.memset(k_g[64:65, hl, :], 1.0),
                     [prev_attn_pe])

        last_kev = {}
        last_sq = ZERO
        for hp in range(2):
            for ts in range(8):
                par = ts % 2
                pq = PA[:, 512 * par:512 * (par + 1)]
                w0 = [twk, tok_lnseq, last_kev.get(par, ZERO), last_sq]
                for kc in range(8):
                    l = wks[:, kc, hp * 128:(hp + 1) * 128]
                    r = ln1_seq[:, kc, ts * 512:(ts + 1) * 512]
                    tkmm = MM(pq, l, r, kc == 0, kc == 7, w0 if kc == 0 else ())
                ksl0 = k_g[0:64, 2 * hp, ts * 512:(ts + 1) * 512]
                ksl1 = k_g[0:64, 2 * hp + 1, ts * 512:(ts + 1) * 512]
                te0 = AC(lambda e, o=ksl0, i=pq[0:64, :]: e.activation(
                    o, i, AF.Copy), [tkmm, prev_attn_pe], big=True)
                te1 = AC(lambda e, o=ksl1, i=pq[64:128, :]: e.activation(
                    o, i, AF.Copy), big=True)
                last_kev[par] = te1
                ts0 = DV(lambda e, i=ksl0: e.scalar_tensor_tensor(
                    sqs[0:64, :], i, 1.0, i, ALU.mult, ALU.mult),
                    [te0, last_sq], big=True)
                ts1 = DV(lambda e, i=ksl1: e.scalar_tensor_tensor(
                    sqs[64:128, :], i, 1.0, i, ALU.mult, ALU.mult), [te1],
                    big=True)
                tqm = MM(PD[0:1, 0:512], ones128[:, :], sqs[:, :], True, True,
                         [ts1, tok_o128])
                col = 16 + hp
                if ts == 0:
                    last_sq = DV(lambda e, col=col: e.reduce_max(
                        sm[0:1, col:col + 1], PD[0:1, 0:512], AX), [tqm])
                else:
                    tr = DV(lambda e: e.reduce_max(
                        sm[0:1, 18:19], PD[0:1, 0:512], AX), [tqm])
                    JD()
                    last_sq = DV(lambda e, col=col: e.tensor_max(
                        sm[0:1, col:col + 1], sm[0:1, 18:19],
                        sm[0:1, col:col + 1]), [tr])
        JD()
        tnb = [DV(lambda e, hp=hp: e.tensor_scalar(
            sm[0:1, 20 + hp:21 + hp], sm[0:1, 16 + hp:17 + hp], -0.5, -1.0,
            ALU.mult, ALU.add), [last_sq]) for hp in range(2)]

        last_qev = {}
        last_qsq = ZERO
        tok_q_done = ZERO
        for hp in range(2):
            for ts in range(4):
                par = ts % 2
                pq = PB[:, 512 * par:512 * (par + 1)]
                w0 = [twq, tok_lnown, last_qev.get(par, ZERO), last_qsq]
                for kc in range(8):
                    l = wqs[:, kc, hp * 128:(hp + 1) * 128]
                    r = ln1_own[:, kc, ts * 512:(ts + 1) * 512]
                    tqmm = MM(pq, l, r, kc == 0, kc == 7, w0 if kc == 0 else ())
                qsl0 = q_g[0:64, 2 * hp, ts * 512:(ts + 1) * 512]
                qsl1 = q_g[0:64, 2 * hp + 1, ts * 512:(ts + 1) * 512]
                te0 = AC(lambda e, o=qsl0, i=pq[0:64, :]: e.activation(
                    o, i, AF.Copy, scale=0.125), [tqmm, prev_attn_pe], big=True)
                te1 = AC(lambda e, o=qsl1, i=pq[64:128, :]: e.activation(
                    o, i, AF.Copy, scale=0.125), big=True)
                last_qev[par] = te1
                ts0 = DV(lambda e, i=qsl0: e.scalar_tensor_tensor(
                    sqs[0:64, :], i, 1.0, i, ALU.mult, ALU.mult),
                    [te0, last_qsq], big=True)
                ts1 = DV(lambda e, i=qsl1: e.scalar_tensor_tensor(
                    sqs[64:128, :], i, 1.0, i, ALU.mult, ALU.mult), [te1],
                    big=True)
                tqm = MM(PD[0:1, 0:512], ones128[:, :], sqs[:, :], True, True,
                         [ts1])
                tb0 = DV(lambda e, hp=hp, ts=ts: e.tensor_scalar(
                    q_g[64:65, 2 * hp, ts * 512:(ts + 1) * 512],
                    PD[0:1, 0:512], -0.5, sm[0:1, 20 + hp:21 + hp],
                    ALU.mult, ALU.add), [tqm, tnb[hp]])
                tb1 = DV(lambda e, hp=hp, ts=ts: e.tensor_scalar(
                    q_g[64:65, 2 * hp + 1, ts * 512:(ts + 1) * 512],
                    PD[0:1, 0:512], -0.5, sm[0:1, 20 + hp:21 + hp],
                    ALU.mult, ALU.add))
                last_qsq = tb1
                tok_q_done = tb1

        if debug and g == 0:
            DMA("sync", dbg_ln[:, :, :], ln1_seq[:, :, :], [tok_lnseq])
            DMA("sync", dbg_lno[:, :, :], ln1_own[:, :, :], [tok_lnown])
            DMA("sync", dbg_k[:, :, :], k_g[:, :, :], [tok_q_done])
            DMA("sync", dbg_q[:, :, :], q_g[:, :, :], [tok_q_done])
        if debug and g == 3:
            DMA("sync", dbg_ln2[:, :, :], ln1_seq[:, :, :], [tok_q_done])

        last_vev = {}
        for tc in range(32):
            par = tc % 2
            pv = PA[:, 512 * par:512 * par + 256]
            w0 = [twv, tok_lnseq, last_vev.get(par, ZERO), last_kev.get(par, ZERO)]
            for kc in range(8):
                l = ln1_seq[:, kc, tc * 128:(tc + 1) * 128]
                r = wvs[:, kc, :]
                tvmm = MM(pv, l, r, kc == 0, kc == 7, w0 if kc == 0 else ())
            vdst = v_g[:, tc, :].rearrange("p (h f) -> p h f", h=4)[:, :, 0:64]
            vsrc = pv.rearrange("p (h f) -> p h f", h=4)
            tvev = DV(lambda e, o=vdst, i=vsrc: e.tensor_copy(o, i),
                      [tvmm, tok_vones, prev_attn_pe], big=True)
            last_vev[par] = tvev

        tok_v_done = last_vev[1]

        tok_oc_sp = prev_spill
        toks_oc_tt = []
        for slot in range(4):
            E = EPAD[slot]
            tmsk = DMA("gpsimd", mkb[:, :, :],
                       masksT[slot].rearrange("u p q -> p u q"),
                       [prev_mask_use])
            for hp in range(2):
                h0 = 2 * hp
                h1 = 2 * hp + 1
                toks_av = []
                for c in range(E):
                    sbuf_i = c % 2
                    PS = [PA, PB][sbuf_i]
                    A2 = a2b[c % 2]
                    wS = [tok_q_done, tok_v_done, last_exp[sbuf_i]]
                    tS0 = MM(PS[:, 0:512],
                             k_g[0:65, h0, c * 128:(c + 1) * 128],
                             q_g[0:65, h0, slot * 512:(slot + 1) * 512],
                             True, True, wS)
                    tS1 = MM(PS[:, 512:1024],
                             k_g[0:65, h1, c * 128:(c + 1) * 128],
                             q_g[0:65, h1, slot * 512:(slot + 1) * 512],
                             True, True)
                    texp = AC(lambda e, A2=A2, PS=PS: e.activation(
                        A2[:, :], PS[:, :], AF.Exp), [tS1, last_a2[c % 2]],
                        big=True)
                    last_exp[sbuf_i] = texp
                    tdep = texp
                    if c >= E - 8:
                        u = c - (E - 8)
                        DV(lambda e, A2=A2, u=u: e.tensor_tensor(
                            A2[:, 0:512], A2[:, 0:512], mkb[:, u, :], ALU.mult),
                            [texp, tmsk], big=True)
                        tdep = DV(lambda e, A2=A2, u=u: e.tensor_tensor(
                            A2[:, 512:1024], A2[:, 512:1024], mkb[:, u, :],
                            ALU.mult), big=True)
                    if debug and g == 0 and slot == 3 and hp == 0 and c == 24:
                        DMA("sync", dbg_a[:, :], A2[:, :], [tdep])
                    wav = [tdep] + ([last_norm[0], last_norm[1]] if c == 0 else [])
                    tA0 = MM(PC[0:65, 0:512],
                             v_g[:, c, h0 * 65:h0 * 65 + 65],
                             A2[:, 0:512], c == 0, c == E - 1, wav)
                    tA1 = MM(PC[0:65, 512:1024],
                             v_g[:, c, h1 * 65:h1 * 65 + 65],
                             A2[:, 512:1024], c == 0, c == E - 1)
                    last_a2[c % 2] = tA1
                    toks_av.append(tA1)
                prev_mask_use = tdep if E >= 8 else prev_mask_use
                for hh in range(2):
                    off = 512 * hh
                    trc = DV(lambda e, off=off: e.reciprocal(
                        rrow[:, :], PC[64:65, off:off + 512]),
                        [toks_av[-1], last_norm[hh]])
                    JD()
                    trb = DV(lambda e: e.tensor_copy(rbf[:, :], rrow[:, :]))
                    trm = MM(PD[0:64, 0:512], ones_c1[0:1, 0:64], rbf[0:1, :],
                             True, True, [trb])
                    tre = AC(lambda e: e.activation(rbc[:, :], PD[0:64, 0:512],
                                                    AF.Copy), [trm], big=True)
                    hloc = 2 * hp + hh
                    dst = oc[64 * hh:64 * hh + 64, hp, slot * 512:(slot + 1) * 512]
                    tno = DV(lambda e, dst=dst, off=off: e.tensor_tensor(
                        dst, PC[0:64, off:off + 512], rbc[:, :], ALU.mult),
                        [tre, tok_oc_sp], big=True)
                    last_norm[hh] = tno
                    toks_oc_tt.append(tno)
        dsto = att_out.rearrange("(c p) q -> p c q", p=128)[:, 2 * g:2 * g + 2, :]
        tok_oc_sp = DMA("sync", dsto, oc[:, :, :], [toks_oc_tt[-1]])
        prev_spill = tok_oc_sp
        prev_attn_pe = toks_av[-1]
        p.barrier()

    tok_att_done = prev_spill
    actx.close()
    lnctx.close()
    p.barrier()

    midctx = contextlib.ExitStack()
    t_fm = midctx.enter_context(nc.sbuf_tensor("t_fm", [128, 8, T_OWN], BF16))

    wctx = contextlib.ExitStack()
    wsb = lambda *a: wctx.enter_context(nc.sbuf_tensor(*a))
    ns_row = wsb("ns_row", [1, D], BF16)
    ns_bc = wsb("ns_bc", [128, D], F32)
    ao_sb = wsb("ao_sb", [128, 8, T_OWN], BF16)
    wout_sb = wsb("wout_sb", [128, 8, D], BF16)
    proj = [wsb(f"proj{i}", [128, D], F32) for i in range(2)]
    wscr = [wsb(f"wscr{i}", [128, D], F32) for i in range(2)]
    xost = [wsb(f"xost{i}", [128, D], F32) for i in range(2)]
    x2st = [wsb(f"x2st{i}", [128, D], F32) for i in range(2)]
    tsc = [wsb(f"tsc{i}", [128, D], F32) for i in range(2)]
    whist = wsb("whist", [128, 16, 8], F32) if debug else None
    tbfs = [wsb(f"tbf{i}", [128, D], BF16) for i in range(2)]
    PP = [wctx.enter_context(nc.psum_tensor(f"PP{i}", [128, 1024], F32))
          for i in range(2)]
    PT = wctx.enter_context(nc.psum_tensor("PT", [128, 1024], BF16))

    tok_ao = DMA("sync", ao_sb[:, :, :],
                 att_out.rearrange("(c p) q -> p c q", p=128), [tok_att_done])
    tok_wout = DMA("gpsimd", wout_sb[:, :, :],
                   wout_t.rearrange("(kc p) n -> p kc n", p=128))
    tok_nsr = DMA("gpsimd", ns_row[:, :], ns[:, :])
    tn1 = MM(PP[0][:, 0:512], ones_c1[:, :], ns_row[:, 0:512], True, True,
             [tok_nsr, tok_o1, tok_att_done])
    tn2 = MM(PP[0][:, 512:1024], ones_c1[:, :], ns_row[:, 512:1024], True, True)
    tok_nsbc = DV(lambda e: e.tensor_copy(ns_bc[:, :], PP[0][:, :]), [tn2])

    last_copy = [tok_nsbc, ZERO]
    last_x2dma = [ZERO, ZERO]
    last_xo = [ZERO, ZERO]
    last_tev = ZERO
    last_chain = [dict(), dict()]
    tok_x2_spills = {}
    for tc in list(range(16)):
        b = tc % 2
        o = 28 * b
        pp = PP[b]
        lc = last_chain[b]
        w0 = [tok_ao, tok_wout, last_copy[b]]
        for kc in range(8):
            l = ao_sb[:, kc, tc * 128:(tc + 1) * 128]
            MM(pp[:, 0:512], l, wout_sb[:, kc, 0:512], kc == 0, False,
               w0 if kc == 0 else ())
            MM(pp[:, 512:1024], l, wout_sb[:, kc, 512:1024], kc == 0, False)
        MM(pp[:, 0:512], ones_c1[0:1, 0:128], bout_sb[0:1, 0:512], False, True,
           [tok_bout])
        tpj = MM(pp[:, 512:1024], ones_c1[0:1, 0:128], bout_sb[0:1, 512:1024],
                 False, True)
        P = proj[b]
        tcp = DV(lambda e, P=P, pp=pp: e.tensor_copy(P[:, :], pp[:, :]),
                 [tpj, lc.get("ap", ZERO)], big=True)
        if debug and tc == 4:
            DMA("sync", dbg_pj[:, :], P[:, :], [tcp])
        last_copy[b] = tcp
        tsum = DV(lambda e, P=P, o=o: e.reduce_sum(sm[:, o:o + 1], P[:, :], AX))
        tsq = DV(lambda e, P=P, S=wscr[b]: e.scalar_tensor_tensor(
            S[:, :], P[:, :], 1.0, P[:, :], ALU.mult, ALU.mult))
        tssq = DV(lambda e, o=o, S=wscr[b]: e.reduce_sum(
            sm[:, o + 1:o + 2], S[:, :], AX), [tsq])
        FENCE("vector")
        tmean = DV(lambda e, o=o: e.tensor_scalar_mul(
            sm[:, o + 2:o + 3], sm[:, o + 0:o + 1], 1.0 / D))
        tmsq = DV(lambda e, o=o: e.scalar_tensor_tensor(
            sm[:, o + 3:o + 4], sm[:, o + 0:o + 1], 1.0 / (D * D),
            sm[:, o + 0:o + 1], ALU.mult, ALU.mult))
        FENCE("vector")
        tvar = DV(lambda e, o=o: e.scalar_tensor_tensor(
            sm[:, o + 4:o + 5], sm[:, o + 1:o + 2], 1.0 / D, sm[:, o + 3:o + 4],
            ALU.mult, ALU.subtract), [lc.get("ln", ZERO)])
        fv3 = FENCE("vector")
        tln = AC(lambda e, o=o: e.activation(
            sm[:, o + 5:o + 6], sm[:, o + 4:o + 5], AF.Ln, bias=eps128[:, 0:1]),
            [fv3])
        FENCE("scalar")
        trs = AC(lambda e, o=o: e.activation(
            sm[:, o + 6:o + 7], sm[:, o + 5:o + 6], AF.Exp, scale=-0.5),
            [lc.get("ap", ZERO)])
        fa2 = FENCE("scalar")
        txo = DMA("sync", xost[b][:, :], x_own[tc * 128:(tc + 1) * 128, :],
                  [last_xo[b]])
        tap = DV(lambda e, P=P, S=wscr[b], o=o: e.tensor_scalar(
            S[:, :], P[:, :], sm[:, o + 2:o + 3], sm[:, o + 6:o + 7],
            ALU.subtract, ALU.mult), [fa2], big=True)
        tx2 = DV(lambda e, S=wscr[b], X2=x2st[b], XO=xost[b]: e.tensor_add(
            X2[:, :], S[:, :], XO[:, :]), [txo, last_x2dma[b]], big=True)
        lc["ap"] = tap
        if debug and tc in (4, 5):
            for ci, cc in enumerate([0, 1, 2, 3, 4, 5, 6]):
                DV(lambda e, tc=tc, ci=ci, cc=cc, o=o: e.tensor_copy(
                    whist[:, tc, ci:ci + 1], sm[:, o + cc:o + cc + 1]), [tt0])
            DMA("sync", dbg_ao[:, :, :], ao_sb[:, :, :], [tok_ao])
        tx2d = DMA("sync", x2s[tc * 128:(tc + 1) * 128, :], x2st[b][:, :], [tx2])
        tok_x2_spills[tc] = tx2d
        last_x2dma[b] = tx2d
        last_xo[b] = tx2
        t2sum = DV(lambda e, X2=x2st[b], o=o: e.reduce_sum(
            sm[:, o + 7:o + 8], X2[:, :], AX), [tx2])
        t2sq = DV(lambda e, X2=x2st[b], S=wscr[b]: e.scalar_tensor_tensor(
            S[:, :], X2[:, :], 1.0, X2[:, :], ALU.mult, ALU.mult))
        t2ssq = DV(lambda e, o=o, S=wscr[b]: e.reduce_sum(
            sm[:, o + 8:o + 9], S[:, :], AX), [t2sq])
        FENCE("vector")
        t2mean = DV(lambda e, o=o, tc=tc: e.tensor_scalar_mul(
            ln2st[:, tc, 0:1], sm[:, o + 7:o + 8], 1.0 / D))
        t2msq = DV(lambda e, o=o: e.scalar_tensor_tensor(
            sm[:, o + 9:o + 10], sm[:, o + 7:o + 8], 1.0 / (D * D),
            sm[:, o + 7:o + 8], ALU.mult, ALU.mult))
        FENCE("vector")
        t2var = DV(lambda e, o=o: e.scalar_tensor_tensor(
            sm[:, o + 10:o + 11], sm[:, o + 8:o + 9], 1.0 / D,
            sm[:, o + 9:o + 10], ALU.mult, ALU.subtract), [lc.get("c2", ZERO)])
        fv3 = FENCE("vector")
        t2ln = AC(lambda e, o=o: e.activation(
            sm[:, o + 11:o + 12], sm[:, o + 10:o + 11], AF.Ln,
            bias=eps128[:, 0:1]), [fv3])
        t2lc = AC(lambda e, o=o: e.activation(
            sm[:, o + 15:o + 16], sm[:, o + 10:o + 11], AF.Ln,
            scale=1.0 + EPS, bias=epsq128[:, 0:1]))
        FENCE("scalar")
        t2rs = AC(lambda e, o=o, tc=tc: e.activation(
            ln2st[:, tc, 1:2], sm[:, o + 11:o + 12], AF.Exp, scale=-0.5))
        tcomb = AC(lambda e, o=o: e.activation(
            sm[:, o + 17:o + 18], sm[:, o + 15:o + 16], AF.Exp, scale=-0.5),
            [lc.get("tt0", ZERO)])
        fa2 = FENCE("scalar")
        lc["ln"] = t2lc
        tt0 = DV(lambda e, X2=x2st[b], T=tsc[b], o=o, tc=tc: e.tensor_scalar(
            T[:, :], X2[:, :], ln2st[:, tc, 0:1], sm[:, o + 17:o + 18],
            ALU.subtract, ALU.mult), [fa2], big=True)
        lc["tt0"] = tt0
        lc["c2"] = tcomb
        ttb = DV(lambda e, T=tsc[b], TB=tbfs[b]: e.tensor_tensor(
            TB[:, :], T[:, :], ns_bc[:, :], ALU.mult), [tok_nsbc, last_tev],
            big=True)
        for kc in range(8):
            lastT = PE(lambda e, TB=tbfs[b], kc=kc: e.transpose(
                PT[:, kc * 128:(kc + 1) * 128],
                TB[:, kc * 128:(kc + 1) * 128], ident_sb[:, :]),
                [ttb, last_tev])
        dst = t_fm[:, :, tc * 128:(tc + 1) * 128]
        srcT = PT.ap().rearrange("p (c q) -> p c q", q=128)
        last_tev = DV(lambda e, dst=dst, srcT=srcT: e.tensor_copy(dst, srcT),
                      [lastT], big=True)
        lc["cb"] = last_tev

    if debug:
        DMA("sync", dbg_ws[:, :], whist[:, 4:6, :].rearrange("p a c -> p (a c)"),
            [last_tev])
        DMA("sync", dbg_t[:, :, :], t_fm[:, :, :], [last_tev])
    tok_t_done = last_tev
    wctx.close()
    p.barrier()

    gctx = contextlib.ExitStack()
    PG = gctx.enter_context(nc.psum_tensor("PG", [128, 512], F32))
    inds = gctx.enter_context(nc.sbuf_tensor("inds", [128, NE], F32))
    ind2s = gctx.enter_context(nc.sbuf_tensor("ind2s", [128, NE], F32))
    msks = gctx.enter_context(nc.sbuf_tensor("msks", [128, NE], F32))
    wtmp = gctx.enter_context(nc.sbuf_tensor("wtmp", [128, NE], F32))

    last_g = ZERO
    for tc in list(range(16)):
        o = 24 + 8 * (tc % 2)
        w0 = [tok_t_done, tok_gw, last_g]
        for kc in range(8):
            MM(PG[:, 0:NE], t_fm[:, kc, tc * 128:(tc + 1) * 128],
               gw_sb[:, kc, :], kc == 0, False, w0 if kc == 0 else ())
        tgm = MM(PG[:, 0:NE], ones_c1[0:1, 0:128], gb_sb[0:1, :], False, True,
                 [tok_gb])
        tm1 = DV(lambda e, o=o: e.reduce_max(sm[:, o:o + 1], PG[:, 0:NE], AX),
                 [tgm, last_g])
        fv1 = FENCE("vector")
        tind = DV(lambda e, o=o: e.tensor_scalar(
            inds[:, :], PG[:, 0:NE], sm[:, o:o + 1], None, ALU.is_ge))
        tnm1 = DV(lambda e, o=o: e.tensor_scalar_mul(
            sm[:, o + 2:o + 3], sm[:, o:o + 1], -1.0))
        fv2 = FENCE("vector")
        tmsk = DV(lambda e: e.scalar_tensor_tensor(
            msks[:, :], inds[:, :], -30000.0, PG[:, 0:NE], ALU.mult, ALU.add))
        fv3 = FENCE("vector")
        tm2 = DV(lambda e, o=o: e.reduce_max(sm[:, o + 1:o + 2], msks[:, :], AX))
        fv4 = FENCE("vector")
        tz = AC(lambda e, o=o: e.activation(
            sm[:, o + 3:o + 4], sm[:, o + 1:o + 2], AF.Exp,
            bias=sm[:, o + 2:o + 3]), [fv4])
        tind2 = DV(lambda e, o=o: e.tensor_scalar(
            ind2s[:, :], msks[:, :], sm[:, o + 1:o + 2], None, ALU.is_ge))
        fa1 = FENCE("scalar")
        tzp = DV(lambda e, o=o: e.tensor_scalar_add(
            sm[:, o + 4:o + 5], sm[:, o + 3:o + 4], 1.0), [fa1])
        fv5 = FENCE("vector")
        tp1 = DV(lambda e, o=o: e.reciprocal(sm[:, o + 5:o + 6], sm[:, o + 4:o + 5]))
        fv6 = FENCE("vector")
        tp2 = DV(lambda e, o=o: e.tensor_mul(
            sm[:, o + 6:o + 7], sm[:, o + 5:o + 6], sm[:, o + 3:o + 4]))
        twt = DV(lambda e, o=o: e.tensor_scalar(
            wtmp[:, :], inds[:, :], sm[:, o + 5:o + 6], None, ALU.mult))
        fv7 = FENCE("vector")
        last_g = DV(lambda e, o=o, tc=tc: e.scalar_tensor_tensor(
            w_st[:, tc, :], ind2s[:, :], sm[:, o + 6:o + 7], wtmp[:, :],
            ALU.mult, ALU.add), [fv7])
    tok_gate = last_g
    if debug:
        fvg = FENCE("vector")
        DMA("sync", dbg_wst[:, :, :], w_st[:, :, :], [fvg])
    gctx.close()
    p.barrier()

    mctx = contextlib.ExitStack()
    msb = lambda *a: mctx.enter_context(nc.sbuf_tensor(*a))
    act_st = msb("act_st", [128, 16, T_OWN], BF16)
    w2_sb = msb("w2_sb", [128, 16, D], BF16)
    w1_sb = [msb(f"w1_sb{i}", [128, 8, 512], BF16) for i in range(2)]
    wbf = msb("wbf", [128, NE], BF16)
    wTs = msb("wTs", [NE, 128], BF16)
    ytc = [msb(f"ytc{i}", [128, D], F32) for i in range(2)]
    gsc = [msb(f"gsc{i}", [128, 512], F32) for i in range(2)]
    ssb = [msb(f"ssb{i}", [128, 512], BF16) for i in range(2)]
    lsb = [msb(f"lsb{i}", [128, 512], BF16) for i in range(2)]
    qsb = [msb(f"qsb{i}", [128, 512], BF16) for i in range(2)]
    PYC = [mctx.enter_context(nc.psum_tensor(f"PYC{i}", [128, 1024], F32))
           for i in range(2)]

    last_yi = [ZERO, ZERO]
    tok_init = []
    FENCE("vector")
    with nc.psum_tensor("PTW", [128, 128], BF16) as PTW:
        last_tr = ZERO
        for tc in range(16):
            b = tc % 2
            twb = DV(lambda e, tc=tc: e.tensor_copy(wbf[:, :], w_st[:, tc, :]),
                     [tok_gate if tc == 0 else tok_init[-1]])
            fvb = FENCE("vector")
            ttr = PE(lambda e: e.transpose(PTW[0:NE, 0:128], wbf[:, :],
                                           ident_sb[:, :]), [fvb, last_tr])
            twt = DV(lambda e: e.tensor_copy(wTs[:, :], PTW[0:NE, 0:128]), [ttr])
            fvc = FENCE("vector")
            last_tr = twt
            m1 = MM(PYC[b][:, 0:512], wTs[0:NE, :], b2_sb[0:NE, 0:512],
                    True, True, [fvc, tok_b2, last_yi[b]])
            m2 = MM(PYC[b][:, 512:1024], wTs[0:NE, :], b2_sb[0:NE, 512:1024],
                    True, True)
            tyc = DV(lambda e, b=b, pp=PYC[b]: e.tensor_copy(
                ytc[b][:, :], pp[:, :]), [m2], big=True)
            last_yi[b] = tyc
            tok_init.append(DMA("gpsimd", y_dram[tc * 128:(tc + 1) * 128, :],
                                ytc[b][:, :], [tyc]))
    tok_init_all = tok_init[-1]
    PHA = mctx.enter_context(nc.psum_tensor("PHA", [128, 1024], F32))
    PHB = mctx.enter_context(nc.psum_tensor("PHB", [128, 1024], F32))

    last_w1 = [ZERO, ZERO]
    last_phB_pe = ZERO
    accum_tc = {tc: tok_init[tc] for tc in range(16)}
    last_ev_g = [ZERO, ZERO]
    last_ev_l = [ZERO, ZERO]
    last_act_tt = [ZERO, ZERO]
    last_ytc = [tok_init[14], tok_init[15]]
    tok_phB_done = ZERO
    tok_accum_last = tok_init_all
    for e_i in range(NE):
        w1_toks = {}
        src4 = w1[e_i].rearrange("(kc p) n -> p kc n", p=128)

        def issue_w1(ipp, guard):
            buf = ipp % 2
            sg = src4[:, :, ipp * 256:(ipp + 1) * 256]
            sl = src4[:, :, 2048 + ipp * 256:2048 + (ipp + 1) * 256]
            DMA("gpsimd", w1_sb[buf][:, :, 0:256], sg, [guard])
            w1_toks[ipp] = DMA("gpsimd", w1_sb[buf][:, :, 256:512], sl)

        issue_w1(0, last_w1[0])
        issue_w1(1, last_w1[1])
        tok_w2 = DMA("gpsimd", w2_sb[:, :, :],
                     w2[e_i].rearrange("(ic p) n -> p ic n", p=128),
                     [last_phB_pe])

        uid = 0
        for ip in range(16):
            ipp, j = ip // 2, ip % 2
            buf = ipp % 2
            W1 = w1_sb[buf]
            for tt in range(4):
                ub = uid % 2
                pg = PHA[:, 512 * (tt % 2):512 * (tt % 2 + 1)]
                pl = PHB[:, 512 * (tt % 2):512 * (tt % 2 + 1)]
                w0 = [w1_toks[ipp], tok_t_done,
                      last_ev_g[tt % 2], last_ev_l[tt % 2]]
                if e_i == 0 and ip == 0 and tt == 0:
                    w0.append(tok_gate)
                    w0.append(tok_init_all)
                if e_i > 0:
                    w0.append(tok_phB_done)
                for kc in range(8):
                    lg = W1[:, kc, j * 128:(j + 1) * 128]
                    rr = t_fm[:, kc, tt * 512:(tt + 1) * 512]
                    MM(pg, lg, rr, kc == 0, kc == 7, w0 if kc == 0 else ())
                for kc in range(8):
                    ll = W1[:, kc, 256 + j * 128:256 + (j + 1) * 128]
                    rr = t_fm[:, kc, tt * 512:(tt + 1) * 512]
                    tml = MM(pl, ll, rr, kc == 0, kc == 7)
                bg = b1c_sb[:, e_i, ip:ip + 1]
                bl = b1c_sb[:, e_i, 16 + ip:17 + ip]
                tg0 = DV(lambda e, pg=pg, G=gsc[ub], bg=bg: e.tensor_scalar(
                    G[:, :], pg, bg, 7.0, ALU.add, ALU.min),
                    [tml, tok_b1p, last_act_tt[ub]], big=True)
                last_ev_g[tt % 2] = tg0
                tgs = AC(lambda e, G=gsc[ub], S=ssb[ub]: e.activation(
                    S[:, :], G[:, :], AF.Gelu_apprx_sigmoid), [tg0], big=True)
                tl0 = AC(lambda e, pl=pl, L=lsb[ub], bl=bl: e.activation(
                    L[:, :], pl, AF.Identity, bias=bl), [tml], big=True)
                last_ev_l[tt % 2] = tl0
                tq = DV(lambda e, L=lsb[ub], Q=qsb[ub]: e.tensor_scalar(
                    Q[:, :], L[:, :], 8.0, -6.0, ALU.min, ALU.max), [tl0],
                    big=True)
                adst = act_st[:, ip, tt * 512:(tt + 1) * 512]
                ta = DV(lambda e, adst=adst, S=ssb[ub], Q=qsb[ub]:
                        e.tensor_tensor(adst, Q[:, :], S[:, :], ALU.mult),
                        [tgs, tq, tok_phB_done], big=True)
                last_act_tt[ub] = ta
                uid += 1
            if j == 1 and ipp + 2 < 8:
                issue_w1(ipp + 2, tml)
        tok_actA = last_act_tt[(uid - 1) % 2]
        last_w1 = [tok_actA, tok_actA]

        for tcm in range(16):
            b = tcm % 2
            py = PYC[b]
            w0 = [tok_w2, tok_actA, last_ytc[b]]
            for ic in range(16):
                l = act_st[:, ic, tcm * 128:(tcm + 1) * 128]
                MM(py[:, 0:512], l, w2_sb[:, ic, 0:512], ic == 0, False,
                   w0 if ic == 0 else ())
                tmB = MM(py[:, 512:1024], l, w2_sb[:, ic, 512:1024],
                         ic == 0, ic == 15)
            last_phB_pe = tmB
            wcol = w_st[:, tcm, e_i:e_i + 1]
            tst = DV(lambda e, py=py, Y=ytc[b], wcol=wcol: e.tensor_scalar(
                Y[:, :], py[:, :], wcol, None, ALU.mult), [tmB, last_ytc[b]],
                big=True)
            if tcm == 15:
                tok_phB_done = tst
            tok_ac = DMA(
                "gpsimd", y_dram[tcm * 128:(tcm + 1) * 128, :], ytc[b][:, :],
                [tst, accum_tc[tcm]], accum_op=ALU.add)
            accum_tc[tcm] = tok_ac
            last_ytc[b] = tok_ac
            tok_accum_last = tok_ac
        p.barrier()
    tok_y_done = tok_accum_last
    mctx.close()
    midctx.close()
    p.barrier()

    fctx = contextlib.ExitStack()
    fsb = lambda *a: fctx.enter_context(nc.sbuf_tensor(*a))
    fx2 = [fsb(f"fx2{i}", [128, D], F32) for i in range(2)]
    fy = [fsb(f"fy{i}", [128, D], F32) for i in range(2)]
    fz = [fsb(f"fz{i}", [128, D], F32) for i in range(2)]
    fs = [fsb(f"fs{i}", [128, D], F32) for i in range(2)]
    fo = [fsb(f"fo{i}", [128, D], F32) for i in range(2)]

    last_fin = [dict(), dict()]
    out_toks = []
    for tc in list(range(16)):
        b = tc % 2
        o = 72 + 12 * b
        lf = last_fin[b]
        tlx = DMA("sync", fx2[b][:, :], x2s[tc * 128:(tc + 1) * 128, :],
                  [tok_x2_spills[tc], lf.get("xo", ZERO)])
        tly = DMA("sync", fy[b][:, :], y_dram[tc * 128:(tc + 1) * 128, :],
                  [tok_y_done, lf.get("ffn", ZERO)])
        tz = DV(lambda e, X=fx2[b], Z=fz[b], tc=tc: e.tensor_scalar(
            Z[:, :], X[:, :], ln2st[:, tc, 0:1], ln2st[:, tc, 1:2],
            ALU.subtract, ALU.mult), [tlx, lf.get("sq", ZERO)])
        tf = DV(lambda e, Z=fz[b], Y=fy[b]: e.tensor_add(
            Z[:, :], Z[:, :], Y[:, :]), [tly], big=True)
        lf["ffn"] = tf
        tsum = DV(lambda e, Z=fz[b], o=o: e.reduce_sum(sm[:, o:o + 1], Z[:, :], AX))
        tsq = DV(lambda e, Z=fz[b], S=fs[b]: e.scalar_tensor_tensor(
            S[:, :], Z[:, :], 1.0, Z[:, :], ALU.mult, ALU.mult))
        lf["sq"] = tsq
        tssq = DV(lambda e, S=fs[b], o=o: e.reduce_sum(
            sm[:, o + 1:o + 2], S[:, :], AX), [tsq])
        FENCE("vector")
        tmean = DV(lambda e, o=o: e.tensor_scalar_mul(
            sm[:, o + 2:o + 3], sm[:, o + 0:o + 1], 1.0 / D))
        tmsq = DV(lambda e, o=o: e.scalar_tensor_tensor(
            sm[:, o + 3:o + 4], sm[:, o + 0:o + 1], 1.0 / (D * D),
            sm[:, o + 0:o + 1], ALU.mult, ALU.mult))
        FENCE("vector")
        tvar = DV(lambda e, o=o: e.scalar_tensor_tensor(
            sm[:, o + 4:o + 5], sm[:, o + 1:o + 2], 1.0 / D, sm[:, o + 3:o + 4],
            ALU.mult, ALU.subtract), [lf.get("ln", ZERO)])
        fv3 = FENCE("vector")
        tln = AC(lambda e, o=o: e.activation(
            sm[:, o + 5:o + 6], sm[:, o + 4:o + 5], AF.Ln, bias=eps128[:, 0:1]),
            [fv3])
        lf["ln"] = tln
        FENCE("scalar")
        trs = AC(lambda e, o=o: e.activation(
            sm[:, o + 6:o + 7], sm[:, o + 5:o + 6], AF.Exp, scale=-0.5),
            [lf.get("ap", ZERO)])
        fa2 = FENCE("scalar")
        tap = DV(lambda e, Z=fz[b], S=fs[b], o=o: e.tensor_scalar(
            S[:, :], Z[:, :], sm[:, o + 2:o + 3], sm[:, o + 6:o + 7],
            ALU.subtract, ALU.mult), [fa2])
        lf["ap"] = tap
        txo = DV(lambda e, S=fs[b], X=fx2[b], O=fo[b]: e.tensor_add(
            O[:, :], S[:, :], X[:, :]), [tap, lf.get("od", ZERO)], big=True)
        lf["xo"] = txo
        tod = DMA("sync", out[tc * 128:(tc + 1) * 128, :], fo[b][:, :], [txo])
        lf["od"] = tod
        out_toks.append(tod)

    p.op("sync", lambda e: e.nop(), [out_toks[-1]])
    fctx.close()

    semnums = sorted(s.num for s in sems.values())
    with nc.Block() as b0:
        @b0.gpsimd
        def _(g):
            g.dma_reset(range(semnums[0], semnums[-1] + 1))
            g.sem_clear(range(semnums[0], semnums[-1] + 1))
    nc.all_engine_barrier()
    p.emit(nc, sems)
    ctx.close()
    return nc


_NC_CACHE = {}


def _get_nc():
    if "nc" not in _NC_CACHE:
        _NC_CACHE["nc"] = build_program()
    return _NC_CACHE["nc"]


def _build_masks(pat):
    m = np.zeros((4, 8, 128, 512), dtype=np.float32)
    for k in range(4):
        cls = CLS[pat][k]
        q0 = 512 * cls
        for u in range(8):
            c = EPAD[k] - 8 + u
            kv = 128 * c + np.arange(128)[:, None]
            q = q0 + np.arange(512)[None, :]
            m[k, u] = (kv <= q).astype(np.float32)
    return m


def prep_in_maps(L0_attn_x, L0_attn_Wqkv, L0_attn_Wout, L0_attn_bout,
                 L0_ffn_norm_scale, L0_ffn_gate_w, L0_ffn_gate_b,
                 L0_ffn_mlp1_w, L0_ffn_mlp1_b, L0_ffn_mlp2_w, L0_ffn_mlp2_b):
    X = np.ascontiguousarray(np.asarray(L0_attn_x, dtype=np.float32))
    Wqkv = np.asarray(L0_attn_Wqkv, dtype=np.float32)
    wq_t = np.ascontiguousarray(Wqkv[0:D].T)
    wk_t = np.ascontiguousarray(Wqkv[D:2 * D].T)
    wv_t = np.ascontiguousarray(Wqkv[2 * D:3 * D].T)
    wout_t = np.ascontiguousarray(np.asarray(L0_attn_Wout, dtype=np.float32).T)
    boutv = np.asarray(L0_attn_bout, dtype=np.float32).reshape(1, D)
    nsv = np.asarray(L0_ffn_norm_scale, dtype=np.float32).reshape(1, D)
    gwv = np.ascontiguousarray(
        np.asarray(L0_ffn_gate_w, dtype=np.float32).reshape(8, 128, NE)
        .transpose(1, 0, 2))
    gbv = np.asarray(L0_ffn_gate_b, dtype=np.float32).reshape(1, NE)
    w1v = np.asarray(L0_ffn_mlp1_w, dtype=np.float32)
    w1_de = np.ascontiguousarray(
        np.concatenate([w1v[:, :, 0::2], w1v[:, :, 1::2]], axis=2))
    b1v = np.asarray(L0_ffn_mlp1_b, dtype=np.float32)
    b1_de = np.concatenate([b1v[:, 0::2], b1v[:, 1::2]], axis=1)
    b1cv = np.ascontiguousarray(
        b1_de.reshape(NE, 32, 128).transpose(0, 2, 1))
    w2v = np.ascontiguousarray(np.asarray(L0_ffn_mlp2_w, dtype=np.float32))
    b2v = np.ascontiguousarray(np.asarray(L0_ffn_mlp2_b, dtype=np.float32))
    identv = np.eye(128, dtype=np.float32)
    masks_by_pat = [_build_masks(0), _build_masks(1)]

    in_maps = []
    for c in range(8):
        pat, s = c // 4, c % 4
        xs = np.ascontiguousarray(X[s])
        xo = np.ascontiguousarray(
            np.concatenate([X[s][cls * 512:(cls + 1) * 512] for cls in CLS[pat]]))
        in_maps.append({
            "x_seq": xs, "x_own": xo,
            "wq_t": wq_t, "wk_t": wk_t, "wv_t": wv_t, "wout_t": wout_t,
            "bout": boutv, "ns": nsv, "gw": gwv, "gb": gbv,
            "w1": w1_de, "b1c": b1cv, "w2": w2v, "b2": b2v,
            "masks": masks_by_pat[pat], "ident": identv,
        })
    return in_maps


def kernel(**inputs):
    in_maps = prep_in_maps(**inputs)
    nc = _get_nc()
    res = run_bass_kernel_spmd(nc, in_maps, core_ids=list(range(8)))
    outp = np.empty((4, T_SEQ, D), dtype=np.float32)
    for c in range(8):
        pat, s = c // 4, c % 4
        oc = res.results[c]["out"]
        for k, cls in enumerate(CLS[pat]):
            outp[s, cls * 512:(cls + 1) * 512] = oc[k * 512:(k + 1) * 512]
    return outp
